# version 36
# baseline (speedup 1.0000x reference)
import math
from contextlib import ExitStack

import numpy as np
import ml_dtypes

import concourse.bass as bass
import concourse.mybir as mybir
from concourse.bass_utils import run_bass_kernel_spmd

F32 = mybir.dt.float32
BF16 = mybir.dt.bfloat16
F8 = mybir.dt.float8e4
AF = mybir.ActivationFunctionType
ALU = mybir.AluOpType
AX = mybir.AxisListType

LAMBDA_INIT = 0.8 - 0.6 * math.exp(-0.3)
ATT_SCALE = 128.0 ** -0.5
NEGBIG = -1.0e30
EPS = 1e-5
FP8_MAX = 448.0
SCALE_EPS = 1e-7


class Cfg:
    def __init__(self, B=16, S=2048, d=2048, H=16, KV=8, n_cores=8):
        self.B, self.S, self.d, self.H, self.KV, self.n_cores = B, S, d, H, KV, n_cores
        self.D = 128
        self.B_loc = B // n_cores
        self.nf = d // 128
        self.DH = H // 2
        self.G = KV // 2
        self.OP = (H + 2 * KV) * 128
        self.NQ = H
        self.NK = KV
        self.NE = H
        self.KVD = KV * 128
        self.NT = S // 128
        self.NT_ALL = self.B_loc * self.NT
        self.QG = 256
        assert self.OP % 256 == 0 and d % 256 == 0 and S % 512 == 0


class Buf:
    __slots__ = ("w", "r")

    def __init__(self, fence=None):
        self.w = list(fence) if fence else []
        self.r = []


class Ctx:
    ENGINES = ("t", "v", "a", "g", "y")

    def __init__(self):
        self.eng_ops = {e: [] for e in self.ENGINES}
        self.cnt = {}
        self.seen = {e: {} for e in self.ENGINES}
        self.semnames = []
        self.stream_last = {}
        self.fence = None

    def sem(self, name):
        if name not in self.cnt:
            self.cnt[name] = 0
            self.semnames.append(name)
        return name

    def buf(self):
        return Buf(self.fence)

    def set_fence(self):
        self.fence = [(s, v) for s, v in self.cnt.items() if v > 0]

    def _waits(self, eng, toks):
        best = {}
        for (s, v) in toks:
            if v > best.get(s, 0):
                best[s] = v
        out = []
        for s, v in best.items():
            if v > self.seen[eng].get(s, 0):
                out.append((s, v))
                self.seen[eng][s] = v
        return out

    def op(self, eng, fns, reads=(), writes=(), stream=None):
        if not isinstance(fns, (list, tuple)):
            fns = [fns]
        toks = []
        for b in reads:
            toks += b.w
        for b in writes:
            toks += b.w + b.r
        waits = self._waits(eng, toks)
        if stream is not None:
            semn = self.sem(stream)
            prev = self.stream_last.get(semn)
            if prev is not None:
                for wv in self._waits(eng, [prev]):
                    waits.append(wv)
            self.cnt[semn] += 16
            tok = (semn, self.cnt[semn])
            self.stream_last[semn] = tok
            amount = 16
        else:
            semn = self.sem("e" + eng)
            self.cnt[semn] += 1
            tok = (semn, self.cnt[semn])
            amount = 1
        self.eng_ops[eng].append((fns, waits, semn, amount))
        for b in reads:
            b.r.append(tok)
        for b in writes:
            b.w = [tok]
            b.r = []
        return tok

    def raw_wait(self, eng, toks):
        waits = self._waits(eng, toks)
        if waits:
            self.eng_ops[eng].append(([None], waits, None, 0))

    def emit(self, eng_obj, eng, handles):
        for fns, waits, semn, amount in self.eng_ops[eng]:
            fns_real = [f for f in fns if f is not None]
            if not fns_real:
                for (s, v) in waits:
                    eng_obj.wait_ge(handles[s], v)
                continue
            for (s, v) in waits[1:]:
                eng_obj.wait_ge(handles[s], v)
            for i, fn in enumerate(fns_real):
                inst = fn(eng_obj)
                if i == 0 and waits:
                    inst._wait_ge(handles[waits[0][0]], waits[0][1])
                if semn is not None and i == len(fns_real) - 1:
                    inst.then_inc(handles[semn], amount)


def _ln_smalls(ctx, d, sum_ap, ssq_ap, var_ap, inv_ap, negminv_ap, eps_ap,
               b_tmp, b_eps, b_keep=None, mean_ap=None, mean_keep=True):
    ALUl, AFl = ALU, AF
    bk = b_keep if b_keep is not None else b_tmp
    bm = bk if mean_keep else b_tmp
    m_ap = mean_ap if mean_ap is not None else sum_ap
    ctx.op("v", (lambda o=m_ap, i=sum_ap, c=1.0 / d:
                 (lambda e: e.tensor_scalar_mul(o, i, c)))(),
           reads=[b_tmp], writes=[bm])
    ctx.op("v", (lambda o=ssq_ap, c=1.0 / d:
                 (lambda e: e.tensor_scalar_mul(o, o, c)))(),
           reads=[b_tmp], writes=[b_tmp])
    ctx.op("v", (lambda o=var_ap, m=m_ap:
                 (lambda e: e.scalar_tensor_tensor(o, m, -1.0, m,
                                                   ALUl.mult, ALUl.mult)))(),
           reads=[bm], writes=[b_tmp])
    ctx.op("v", (lambda o=var_ap, i=ssq_ap:
                 (lambda e: e.tensor_add(o, o, i)))(),
           reads=[b_tmp], writes=[b_tmp])
    ctx.op("a", (lambda o=inv_ap, i=var_ap, b=eps_ap:
                 (lambda e: e.activation(out=o, in_=i, func=AFl.Ln, bias=b)))(),
           reads=[b_tmp, b_eps], writes=[bk])
    ctx.op("a", (lambda o=inv_ap:
                 (lambda e: e.activation(out=o, in_=o, func=AFl.Exp, scale=-0.5)))(),
           reads=[bk], writes=[bk])
    if negminv_ap is not None:
        ctx.op("v", (lambda o=negminv_ap, m=m_ap, i=inv_ap:
                     (lambda e: e.scalar_tensor_tensor(o, m, -1.0, i,
                                                       ALUl.mult, ALUl.mult)))(),
               reads=[bk, bm], writes=[bk])


def build(cfg: Cfg):
    nc = bass.Bass("TRN2", target_bir_lowering=False, num_swdge_queues=4)
    S, d, nf = cfg.S, cfg.d, cfg.nf
    NT, QG = cfg.NT, cfg.QG

    xin = nc.dram_tensor("xin", [cfg.B_loc, S, d], F32, kind="ExternalInput").ap()
    xin_fm = nc.dram_tensor("xin_fm", [cfg.B_loc, d, S], F32,
                            kind="ExternalInput").ap()
    qkvb_row = nc.dram_tensor("qkvb_row", [1, cfg.OP], F32,
                              kind="ExternalInput").ap()
    wqkvT = nc.dram_tensor("wqkvT", [d, cfg.OP], F32, kind="ExternalInput").ap()
    qkvb_t = nc.dram_tensor("qkvb_t", [128, cfg.OP // 128], F32, kind="ExternalInput").ap()
    biasv_row = nc.dram_tensor("biasv_row", [1, cfg.KVD], F32, kind="ExternalInput").ap()
    lam4 = nc.dram_tensor("lam4", [128, 4], F32, kind="ExternalInput").ap()
    subln2 = nc.dram_tensor("subln2", [128, 2], F32, kind="ExternalInput").ap()
    outwT = nc.dram_tensor("outwT", [cfg.NE * 128, d], F32, kind="ExternalInput").ap()
    outb_row = nc.dram_tensor("outb_row", [1, d], F32, kind="ExternalInput").ap()
    ffnT = nc.dram_tensor("ffnT", [d, d], F32, kind="ExternalInput").ap()
    ffnb_row = nc.dram_tensor("ffnb_row", [1, d], F32, kind="ExternalInput").ap()
    ws_in = nc.dram_tensor("ws_in", [1, 1], F32, kind="ExternalInput").ap()
    xout = nc.dram_tensor("xout", [cfg.B_loc, S, d], F32, kind="ExternalOutput").ap()

    qspill = nc.dram_tensor("qspill", [cfg.B_loc, cfg.NQ, 128, S], BF16).ap()
    kspill = nc.dram_tensor("kspill", [cfg.B_loc, cfg.NK, 128, S], BF16).ap()
    vspill = nc.dram_tensor("vspill", [cfg.B_loc, cfg.G, cfg.NT, 128, 256], BF16).ap()
    xmid_d = nc.dram_tensor("xmid_d", [cfg.NT_ALL, 128, d], F32).ap()
    cc_in = nc.dram_tensor("cc_in", [1, 1], F32)
    cc_out = nc.dram_tensor("cc_out", [1, 1], F32)

    id_bf_h = nc.inline_tensor(np.eye(128, dtype=ml_dtypes.bfloat16), "id_bf")
    id_f32_h = nc.inline_tensor(np.eye(128, dtype=np.float32), "id_f32")
    ut2 = np.zeros((128, 128), dtype=ml_dtypes.bfloat16)
    for dd in range(128):
        ut2[dd, : dd + 1] = 1.0
    ut2_h = nc.inline_tensor(ut2, "ut2")
    R_OFF = [0, 128, 384, 768]
    r4 = np.zeros((128, 1280), dtype=ml_dtypes.bfloat16)
    for a in range(4):
        for j in range(128 * a + 1, 128 * (a + 1)):
            r4[j - 128 * a - 1, R_OFF[a] + j] = NEGBIG
    r4_h = nc.inline_tensor(r4, "r4")
    ones_h = nc.inline_tensor(np.ones((128, 128), dtype=np.float32), "onesf")
    onesbf_h = nc.inline_tensor(np.ones((128, 2), dtype=ml_dtypes.bfloat16), "onesbf")
    onesr_h = nc.inline_tensor(np.ones((1, 512), dtype=ml_dtypes.bfloat16), "onesr")

    ctx = Ctx()
    es = ExitStack()
    with es:
        sb = lambda name, shape, dt: es.enter_context(nc.sbuf_tensor(name, shape, dt))
        id_bf = sb("c_idbf", [128, 128], BF16)
        id_f32 = sb("c_idf32", [128, 128], F32)
        utb = sb("c_ut2", [128, 128], BF16)
        r4b = sb("c_r4", [128, 1280], BF16)
        onesb = sb("c_ones", [128, 128], F32)
        onesbf = sb("c_onesbf", [128, 2], BF16)
        onesr = sb("c_onesr", [1, 512], BF16)
        qkvbb = sb("c_qkvb", [128, cfg.OP // 128], F32)
        lamb = sb("c_lam4", [128, 4], F32)
        sub2 = sb("c_sub2", [128, 2], F32)
        wCb = sb("c_wC", [128, 2], F32)
        epsb = sb("c_eps", [128, 1], F32)
        neglam_b = sb("c_neglam", [128, 1], F32)
        smallb = sb("c_smallb", [1, 8], F32)
        g2 = sb("c_g2", [128, 2], F32)
        wssb = sb("c_ws", [1, 1], F32)
        m2s = sb("s_mean2", [128, cfg.NT_ALL], F32)
        i2s = sb("s_inv2", [128, cfg.NT_ALL], F32)
        absm = sb("s_absm", [128, cfg.NT_ALL], F32)
        stats = [[sb(f"st{i}_{p}", [128, 1], F32) for i in range(6)]
                 for p in range(2)]

        B_ = lambda: ctx.buf()
        b_consts = B_()
        b_eps = B_()
        b_stats = B_()
        b_small = B_()
        b_stp = [B_(), B_()]
        b_qspill = B_()
        b_xmid = [B_() for _ in range(cfg.NT_ALL)]
        b_g2 = B_()
        b_out = B_()

        def dma(eng_name, stream, outap, inap, reads=(), writes=()):
            return ctx.op(eng_name,
                          (lambda o=outap, i=inap:
                           (lambda e: e.dma_start(out=o, in_=i)))(),
                          reads=reads, writes=writes, stream=stream)

        dma("y", "dc0", id_bf[:], id_bf_h.ap(), writes=[b_consts])
        dma("y", "dc0", id_f32[:], id_f32_h.ap(), writes=[b_consts])
        dma("y", "dc0", utb[:], ut2_h.ap(), writes=[b_consts])
        dma("y", "dc0", r4b[:], r4_h.ap(), writes=[b_consts])
        dma("y", "dc0", onesb[:], ones_h.ap(), writes=[b_consts])
        dma("y", "dc0", onesbf[:], onesbf_h.ap(), writes=[b_consts])
        dma("y", "dc0", onesr[:], onesr_h.ap(), writes=[b_consts])
        dma("y", "dc0", qkvbb[:], qkvb_t, writes=[b_consts])
        dma("y", "dc0", lamb[:], lam4, writes=[b_consts])
        dma("y", "dc0", sub2[:], subln2, writes=[b_consts])
        dma("y", "dc0", wssb[:], ws_in, writes=[b_consts])

        ctx.op("v", lambda e: e.memset(epsb[:], EPS), writes=[b_eps])
        ctx.op("v", lambda e: e.tensor_scalar_mul(wCb[:], sub2[:], 1.0 - LAMBDA_INIT),
               reads=[b_consts], writes=[b_small])

        def mm(ps_ap, lhsT_ap, rhs_ap, start, stop):
            return (lambda p=ps_ap, l=lhsT_ap, r=rhs_ap, s1=start, s2=stop:
                    (lambda e: e.matmul(p, l, r, start=s1, stop=s2)))()

        def tr(ps_ap, in_ap, ident):
            return (lambda p=ps_ap, i=in_ap, d2=ident:
                    (lambda e: e.transpose(p, i, d2)))()

        def act(out_ap, in_ap, func, bias=0.0, scale=1.0, accum=None):
            return (lambda o=out_ap, i=in_ap, f=func, b=bias, s=scale, ac=accum:
                    (lambda e: e.activation(out=o, in_=i, func=f, bias=b, scale=s,
                                            accum_out=ac)))()

        def stt(out_ap, in0, scalar, in1, op0, op1, accum=None):
            return (lambda o=out_ap, a0=in0, sc=scalar, a1=in1, p0=op0, p1=op1, ac=accum:
                    (lambda e: e.scalar_tensor_tensor(o, a0, sc, a1, p0, p1,
                                                      accum_out=ac)))()

        def ts(out_ap, in0, s1, s2, op0, op1=ALU.bypass, accum=None):
            return (lambda o=out_ap, a0=in0, x1=s1, x2=s2, p0=op0, p1=op1, ac=accum:
                    (lambda e: e.tensor_scalar(o, a0, x1, x2, p0, p1, accum_out=ac)))()

        def red(out_ap, in_ap, op, absval=False):
            return (lambda o=out_ap, i=in_ap, p=op, av=absval:
                    (lambda e: e.tensor_reduce(o, i, AX.X, p,
                                               apply_absolute_value=av)))()

        def cp(out_ap, in_ap):
            return (lambda o=out_ap, i=in_ap: (lambda e: e.tensor_copy(o, i)))()

        with nc.psum_tensor("ps_lam", [128, 8], F32) as ps_lam:
            b_pl = B_()
            ctx.op("t", [mm(ps_lam[0:1, 0:1], lamb[:, 0:1], lamb[:, 1:2], True, True),
                         mm(ps_lam[0:1, 1:2], lamb[:, 2:3], lamb[:, 3:4], True, True)],
                   reads=[b_consts], writes=[b_pl])
            ctx.op("a", act(smallb[0:1, 0:2], ps_lam[0:1, 0:2], AF.Exp),
                   reads=[b_pl], writes=[b_small])
            ctx.op("v", (lambda e: e.tensor_sub(smallb[0:1, 2:3], smallb[0:1, 0:1],
                                                smallb[0:1, 1:2])),
                   reads=[b_small], writes=[b_small])
            ctx.op("v", ts(smallb[0:1, 3:4], smallb[0:1, 2:3], LAMBDA_INIT, -1.0,
                           ALU.add, ALU.mult),
                   reads=[b_small], writes=[b_small])
            ctx.op("t", mm(ps_lam[:, 2:3], onesb[0:1, :], smallb[0:1, 3:4], True, True),
                   reads=[b_small], writes=[b_pl])
            ctx.op("v", cp(neglam_b[:], ps_lam[:, 2:3]), reads=[b_pl], writes=[b_small])

        for seq in range(cfg.B_loc):
            ctx.set_fence()
            actA_g = nc.sbuf_tensor(f"actA_{seq}", [128, max(nf, cfg.NE) * S], BF16)
            actA = actA_g.__enter__()
            with ExitStack() as seq_es:
                ssb = lambda name, shape, dt: seq_es.enter_context(
                    nc.sbuf_tensor(name + f"_s{seq}", shape, dt))
                b_actA = B_()
                b_kspill, b_vspill = B_(), B_()
                actA3 = actA[:, 0:nf * S].rearrange("p (n c) -> p n c", c=S)

                ctx.set_fence()
                with ExitStack() as ab_es:
                    asb = lambda name, shape, dt: ab_es.enter_context(
                        nc.sbuf_tensor(name + f"_ab{seq}", shape, dt))
                    bufW = [asb("bufW0", [128, nf * 256], BF16),
                            asb("bufW1", [128, nf * 256], BF16)]
                    qst = [asb(f"qst{i}", [128, 512], BF16) for i in range(4)]
                    vst = [asb(f"vst{i}", [128, 512], BF16) for i in range(4)]
                    qtm = [asb(f"qtm{i}", [128, 512], F32) for i in range(4)]
                    vtm = [asb("vtm0", [128, 512], F32),
                           asb("vtm1", [128, 512], F32)]
                    vbw = asb("vbw", [128, nf * 512], BF16)
                    tmt = [asb("tmt0", [128, d], BF16), asb("tmt1", [128, d], BF16)]
                    xt = [asb(f"xt{i}", [128, d], F32) for i in range(2)]
                    xfs = [asb(f"xfs{i}", [128, S], F32) for i in range(2)]
                    i1s = asb("i1s", [128, NT], F32)
                    nm1s = asb("nm1s", [128, NT], F32)
                    i1b = asb("i1b", [128, NT], BF16)
                    nm1b = asb("nm1b", [128, NT], BF16)
                    binvF = asb("binvF", [128, S], BF16)
                    bnmF = asb("bnmF", [128, S], BF16)
                    rwvb = asb("rwvb", [128, 4], BF16)
                    rsvrow = asb("rsvrow", [1, 512], F32)
                    rsWvb = asb("rsWvb", [128, 512], F32)
                    rsW_sb = asb("rsW", [128, 32], F32)
                    rowst = asb("rowst", [1, 256], F32)
                    bqrow = asb("bqrow", [1, cfg.OP], BF16)
                    b_bqr = B_()
                    dma("g", "dw0", bqrow[:], qkvb_row, writes=[b_bqr])
                    b_vbw = B_()
                    b_xt = [B_(), B_()]
                    b_xfs = [B_(), B_()]
                    b_xfm = [B_() for _ in range(nf)]
                    b_st1 = [B_() for _ in range(NT)]
                    b_bufW = [B_(), B_()]
                    b_qst = [B_() for _ in range(4)]
                    b_vst = [B_() for _ in range(4)]
                    b_qtm = [B_() for _ in range(4)]
                    b_vtm = [B_(), B_()]
                    b_tmt = [B_(), B_()]
                    b_binvF = B_()
                    b_rsv = B_()
                    b_rsW = [B_() for _ in range(4)]
                    b_rowst = B_()

                    for ic in range(nf):
                        fp = ic % 2
                        dma("a", f"dfa{fp}", xfs[fp][:],
                            xin_fm[seq, ic * 128:(ic + 1) * 128, :],
                            writes=[b_xfs[fp]])
                        ctx.op("v", cp(actA3[:, ic, :], xfs[fp][:]),
                               reads=[b_xfs[fp]],
                               writes=[b_xfm[ic], b_actA])

                    def a_stats(t):
                        par, lp = t % 2, t % 2
                        st = stats[par]
                        bsp = b_stp[par]
                        dma("y", f"dx{lp}", xt[lp][:],
                            xin[seq, t * 128:(t + 1) * 128, :], writes=[b_xt[lp]])
                        ctx.op("a", act(tmt[par][:], xt[lp][:], AF.Identity,
                                        accum=st[0][:]),
                               reads=[b_xt[lp]], writes=[b_tmt[par], bsp])
                        ctx.op("a", act(tmt[par][:], xt[lp][:], AF.Square,
                                        accum=st[1][:]),
                               reads=[b_xt[lp]], writes=[b_tmt[par], bsp])
                        _ln_smalls(ctx, d, st[0][:], st[1][:], st[2][:],
                                   i1s[:, t:t + 1], nm1s[:, t:t + 1], epsb[:],
                                   bsp, b_eps, b_keep=b_st1[t],
                                   mean_ap=st[5][:], mean_keep=False)
                        ctx.op("v", cp(i1b[:, t:t + 1], i1s[:, t:t + 1]),
                               reads=[b_st1[t]], writes=[b_st1[t]])
                        ctx.op("v", cp(nm1b[:, t:t + 1], nm1s[:, t:t + 1]),
                               reads=[b_st1[t]], writes=[b_st1[t]])

                    ctx.set_fence()
                    with ExitStack() as ps_es:
                        psB = [ps_es.enter_context(
                            nc.psum_tensor(f"psB{i}_{seq}", [128, 512], F32))
                            for i in range(8)]
                        b_psB = [B_() for _ in range(8)]

                        def a_bcast(t):
                            ctx.op("t", [mm(psB[3][0:1, 0:128], i1b[:, t:t + 1],
                                            id_bf[:], True, True),
                                         mm(psB[3][0:1, 128:256],
                                            nm1b[:, t:t + 1],
                                            id_bf[:], True, True)],
                                   reads=[b_st1[t], b_consts],
                                   writes=[b_psB[3]])
                            ctx.op("v", cp(rowst[0:1, 0:256],
                                           psB[3][0:1, 0:256]),
                                   reads=[b_psB[3]], writes=[b_rowst])
                            ctx.op("t", [mm(psB[3][:, 0:128], onesb[0:1, :],
                                            rowst[0:1, 0:128], True, True),
                                         mm(psB[3][:, 128:256], onesb[0:1, :],
                                            rowst[0:1, 128:256], True, True)],
                                   reads=[b_rowst, b_consts],
                                   writes=[b_psB[3]])
                            ctx.op("v", cp(binvF[:, t * 128:(t + 1) * 128],
                                           psB[3][:, 0:128]),
                                   reads=[b_psB[3]], writes=[b_binvF])
                            ctx.op("v", cp(bnmF[:, t * 128:(t + 1) * 128],
                                           psB[3][:, 128:256]),
                                   reads=[b_psB[3]], writes=[b_binvF])

                        for t in range(NT):
                            a_stats(t)
                        bc_state = [0]

                        def bcast_upto(k):
                            while bc_state[0] <= min(k, NT - 1):
                                a_bcast(bc_state[0])
                                bc_state[0] += 1

                        nslab = cfg.OP // 256
                        nqk_slab = (cfg.NQ + cfg.NK) // 2
                        nvs = nslab - nqk_slab

                        def rsv_aux(wsrc, b_wsrc, width):
                            nch = width // 128
                            for cc in range(nch):
                                ctx.op("t", [mm(psB[2][:, cc:cc + 1],
                                                wsrc[:, ic * width + cc * 128:
                                                     ic * width + cc * 128 + 128],
                                                onesbf[:, 0:1],
                                                ic == 0, ic == nf - 1)
                                             for ic in range(nf)],
                                       reads=[b_wsrc, b_consts],
                                       writes=[b_psB[2]])
                            ctx.op("v", cp(rwvb[:, 0:nch], psB[2][:, 0:nch]),
                                   reads=[b_psB[2]], writes=[b_rsv])
                            ctx.op("t", [mm(psB[2][0:1, cc * 128:(cc + 1) * 128],
                                            rwvb[:, cc:cc + 1], id_bf[:],
                                            True, True) for cc in range(nch)],
                                   reads=[b_rsv, b_consts], writes=[b_psB[2]])
                            ctx.op("v", cp(rsvrow[0:1, 0:width],
                                           psB[2][0:1, 0:width]),
                                   reads=[b_psB[2]], writes=[b_rowst])
                            ctx.op("t", mm(psB[2][:, 0:width], onesb[0:1, :],
                                           rsvrow[0:1, 0:width], True, True),
                                   reads=[b_rowst, b_consts],
                                   writes=[b_psB[2]])
                            ctx.op("v", cp(rsWvb[:, 0:width],
                                           psB[2][:, 0:width]),
                                   reads=[b_psB[2]], writes=[b_rsv])

                        def v_tile(t, wsrc, b_wsrc, width, voff, split):
                            pv = psB[t % 2]
                            bpv = b_psB[t % 2]
                            vp = t % 2
                            if split:
                                for ic in range(nf):
                                    ctx.op("t", mm(pv[:, 0:width],
                                                   actA[:, ic * S + t * 128:
                                                        ic * S + (t + 1) * 128],
                                                   wsrc[:, ic * width:
                                                        (ic + 1) * width],
                                                   ic == 0, False),
                                           reads=[b_wsrc, b_xfm[ic]],
                                           writes=[bpv])
                            else:
                                ctx.op("t", [mm(pv[:, 0:width],
                                                actA[:, ic * S + t * 128:
                                                     ic * S + (t + 1) * 128],
                                                wsrc[:, ic * width:
                                                     (ic + 1) * width],
                                                ic == 0, False)
                                             for ic in range(nf)],
                                       reads=[b_wsrc, b_actA], writes=[bpv])
                            ctx.op("t", mm(pv[:, 0:width], onesr[0:1, 0:128],
                                           bqrow[0:1, voff:voff + width],
                                           False, True),
                                   reads=[b_consts, b_bqr], writes=[bpv])
                            ctx.op("v", ts(vtm[vp][:, 0:width], pv[:, 0:width],
                                           i1s[:, t:t + 1], None, ALU.mult),
                                   reads=[bpv, b_st1[t]], writes=[b_vtm[vp]])
                            vs = t % 4
                            ctx.op("v", stt(vst[vs][:, 0:width],
                                            rsWvb[:, 0:width], nm1s[:, t:t + 1],
                                            vtm[vp][:, 0:width],
                                            ALU.mult, ALU.add),
                                   reads=[b_vtm[vp], b_st1[t], b_rsv],
                                   writes=[b_vst[vs]])
                            return vs

                        for g4v in range(0, nvs, 2):
                            s_i = nqk_slab + g4v
                            par = s_i % 2
                            if g4v + 1 < nvs:
                                srcv = wqkvT[:, s_i * 256:(s_i + 2) * 256] \
                                    .rearrange("(c p) o -> p c o", p=128)
                                dma("g", f"dw{par}",
                                    vbw[:].rearrange("p (c o) -> p c o", o=512),
                                    srcv, writes=[b_vbw])
                                rsv_aux(vbw, b_vbw, 512)
                                for t in range(NT):
                                    vp = v_tile(t, vbw, b_vbw, 512, s_i * 256,
                                                g4v == 0 and t < 6)
                                    vdst = vspill[seq, g4v:g4v + 2, t, :, :] \
                                        .rearrange("g p v -> p g v")
                                    dma("y", f"dq{vp}", vdst,
                                        vst[vp][:].rearrange(
                                            "p (g v) -> p g v", v=256),
                                        reads=[b_vst[vp]],
                                        writes=[b_vspill])
                                    bcast_upto(t - 3 + (0 if g4v == 0 else NT))
                            else:
                                srcv = wqkvT[:, s_i * 256:(s_i + 1) * 256] \
                                    .rearrange("(c p) o -> p c o", p=128)
                                dma("g", f"dw{par}",
                                    bufW[par][:].rearrange(
                                        "p (c o) -> p c o", o=256),
                                    srcv, writes=[b_bufW[par]])
                                rsv_aux(bufW[par], b_bufW[par], 256)
                                for t in range(NT):
                                    vp = v_tile(t, bufW[par], b_bufW[par], 256,
                                                s_i * 256, g4v == 0 and t < 6)
                                    dma("y", f"dq{vp}",
                                        vspill[seq, g4v, t, :, :],
                                        vst[vp][:, 0:256],
                                        reads=[b_vst[vp]],
                                        writes=[b_vspill])
                                    bcast_upto(t - 3 + (0 if g4v == 0 else NT))
                        bcast_upto(NT - 1)

                        for s_i in range(nqk_slab):
                            par = s_i % 2
                            src = wqkvT[:, s_i * 256:(s_i + 1) * 256] \
                                .rearrange("(c p) o -> p c o", p=128)
                            dst_w = bufW[par][:].rearrange(
                                "p (c o) -> p c o", o=256)
                            dma("g", f"dw{par}", dst_w, src,
                                writes=[b_bufW[par]])
                            for j in range(2):
                                oc = 2 * s_i + j
                                base = 4 * (oc % 2)
                                grp = [psB[base + q] for q in range(4)]
                                bgrp = [b_psB[base + q] for q in range(4)]
                                ctx.op("t", [mm(psB[base + 3][:, 511:512],
                                                bufW[par][:, ic * 256 + j * 128:
                                                          ic * 256 + j * 128
                                                          + 128],
                                                onesbf[:, 0:1],
                                                ic == 0, ic == nf - 1)
                                             for ic in range(nf)],
                                       reads=[b_bufW[par], b_consts],
                                       writes=[b_psB[base + 3]])
                                ctx.op("a", act(rsW_sb[:, oc:oc + 1],
                                                psB[base + 3][:, 511:512],
                                                AF.Identity),
                                       reads=[b_psB[base + 3]],
                                       writes=[b_rsW[oc % 4]])
                                fns = []
                                for ic in range(nf):
                                    for t4 in range(S // 512):
                                        fns.append(mm(
                                            grp[t4][:],
                                            bufW[par][:, ic * 256 + j * 128:
                                                      ic * 256 + j * 128 + 128],
                                            actA[:, ic * S + t4 * 512:
                                                 ic * S + (t4 + 1) * 512],
                                            ic == 0, False))
                                for t4 in range(S // 512):
                                    fns.append(mm(
                                        grp[t4][:],
                                        bqrow[0:1, oc * 128:(oc + 1) * 128],
                                        onesr[0:1, 0:512], False, True))
                                ctx.op("t", fns,
                                       reads=[b_bufW[par], b_actA, b_consts,
                                              b_bqr],
                                       writes=bgrp)
                                for t4 in range(S // 512):
                                    ctx.op("v", (lambda o=qtm[t4][:],
                                                 i0=grp[t4][:],
                                                 i1=binvF[:, t4 * 512:
                                                          (t4 + 1) * 512]:
                                                 (lambda e: e.tensor_tensor(
                                                     o, i0, i1, ALU.mult)))(),
                                           reads=[bgrp[t4], b_binvF],
                                           writes=[b_qtm[t4]])
                                for t4 in range(S // 512):
                                    ctx.op("v", stt(qst[t4][:],
                                                    bnmF[:, t4 * 512:
                                                         (t4 + 1) * 512],
                                                    rsW_sb[:, oc:oc + 1],
                                                    qtm[t4][:],
                                                    ALU.mult, ALU.add),
                                           reads=[b_qtm[t4], b_binvF,
                                                  b_rsW[oc % 4]],
                                           writes=[b_qst[t4]])
                                    if oc < cfg.NQ:
                                        dma("y", f"dq{t4}",
                                            qspill[seq, oc, :,
                                                   t4 * 512:(t4 + 1) * 512],
                                            qst[t4][:], reads=[b_qst[t4]],
                                            writes=[b_qspill])
                                    else:
                                        dma("y", f"dq{t4}",
                                            kspill[seq, oc - cfg.NQ, :,
                                                   t4 * 512:(t4 + 1) * 512],
                                            qst[t4][:], reads=[b_qst[t4]],
                                            writes=[b_kspill])

                ctx.set_fence()
                with ExitStack() as c_es:
                    csb = lambda name, shape, dt: c_es.enter_context(
                        nc.sbuf_tensor(name + f"_c{seq}", shape, dt))
                    ckp = [csb("ck0", [128, 2 * S], BF16),
                           csb("ck1", [128, 2 * S], BF16)]
                    cvp = [csb("cv0", [128, NT * 256], BF16),
                           csb("cv1", [128, NT * 256], BF16)]
                    cq = [csb(f"cq{i}", [128, 2 * S], BF16) for i in range(4)]
                    e1 = [[csb(f"e1_{L}{q}", [128, S], BF16) for q in range(2)]
                          for L in range(2)]
                    e2 = [[csb(f"e2_{L}{q}", [128, S], BF16) for q in range(2)]
                          for L in range(2)]
                    pbuf = [[csb(f"pb_{L}{q}", [128, S], BF16) for q in range(2)]
                            for L in range(2)]
                    bufPT = [csb("bufPT0", [128, NT * QG], BF16),
                             csb("bufPT1", [128, NT * QG], BF16)]
                    osq = [csb("osq0", [128, 2 * QG], BF16),
                           csb("osq1", [128, 2 * QG], BF16)]
                    o_sb = [csb("osb0", [128, 2 * QG], BF16),
                            csb("osb1", [128, 2 * QG], BF16)]
                    S_sb = [csb("S_sb0", [128, QG], F32),
                            csb("S_sb1", [128, QG], F32)]
                    zb = [[csb(f"zb{L}{q}", [128, 8], F32) for q in range(2)]
                          for L in range(2)]
                    zs = [[csb(f"zs{L}{q}", [128, 2], F32) for q in range(2)]
                          for L in range(2)]
                    zr = [[csb(f"zr{L}{q}", [128, 2], F32) for q in range(2)]
                          for L in range(2)]
                    a2s = [[csb(f"a2{L}{q}", [128, 1], F32) for q in range(2)]
                           for L in range(2)]
                    ez = [csb(f"ez{L}", [128, NT], BF16) for L in range(2)]
                    b_ckp = [B_(), B_()]
                    b_cvp = [B_(), B_()]
                    b_cq = [B_() for _ in range(4)]
                    b_e1 = [[B_(), B_()], [B_(), B_()]]
                    b_e2 = [[B_(), B_()], [B_(), B_()]]
                    b_pb = [[B_(), B_()], [B_(), B_()]]
                    b_PT = [B_(), B_()]
                    b_osq = [B_(), B_()]
                    b_osb = [B_(), B_()]
                    b_Ssb = [B_(), B_()]
                    b_z = [[B_(), B_()], [B_(), B_()]]
                    b_zs = [[B_(), B_()], [B_(), B_()]]
                    b_ez = [[B_() for _ in range(NT)] for _ in range(2)]
                    bufPT3 = [bufPT[i][:].rearrange("p (n c) -> p n c", c=QG)
                              for i in range(2)]

                    with ExitStack() as ps_es:
                        psS = [ps_es.enter_context(
                            nc.psum_tensor(f"psS{i}_{seq}", [128, 1024], F32))
                            for i in range(2)]
                        psO = [ps_es.enter_context(
                            nc.psum_tensor(f"psO{i}_{seq}", [128, 2 * QG], F32))
                            for i in range(2)]
                        psPT = [ps_es.enter_context(
                            nc.psum_tensor(f"psPT{i}_{seq}", [128, 1024], BF16))
                            for i in range(2)]
                        b_psS = [[B_(), B_()], [B_(), B_()]]
                        b_psO = [B_(), B_()]
                        b_psPT = [B_(), B_()]

                        def kv_load(hp):
                            pp = hp % 2
                            ksrc = kspill[seq, 2 * hp:2 * hp + 2, :, :] \
                                .rearrange("c p t -> p c t")
                            dma("y", "dql0",
                                ckp[pp][:].rearrange("p (c t) -> p c t", t=S),
                                ksrc, reads=[b_kspill], writes=[b_ckp[pp]])
                            vsrc = vspill[seq, hp, :, :, :] \
                                .rearrange("t p v -> p t v")
                            dma("y", "dql1",
                                cvp[pp][:].rearrange("p (t v) -> p t v", v=256),
                                vsrc, reads=[b_vspill], writes=[b_cvp[pp]])

                        def q_load(h):
                            qi = (h % 2) * 2 + (h // 2) % 2
                            src = qspill[seq, 2 * h:2 * h + 2, :, :] \
                                .rearrange("c p t -> p c t")
                            dma("y", "dql2",
                                cq[qi][:].rearrange("p (c t) -> p c t", t=S),
                                src, reads=[b_qspill], writes=[b_cq[qi]])

                        def scores_pair(hp, qt):
                            pp = hp % 2
                            kmax = (qt + 1) * 128
                            nfull = (qt * 128) // 512
                            nkc = nfull + 1
                            aa = qt % 4
                            wrem = kmax - nfull * 512
                            qp = qt % 2
                            for kc in range(nkc):
                                w0 = 512 if kc < nfull else wrem
                                for L in range(2):
                                    qi = L * 2 + hp % 2
                                    for comp in range(2):
                                        eX = e1[L][qp] if comp == 0 else e2[L][qp]
                                        bX = (b_e1[L][qp] if comp == 0
                                              else b_e2[L][qp])
                                        lhq = cq[qi][:, comp * S + qt * 128:
                                                     comp * S + (qt + 1) * 128]
                                        kbase = comp * S
                                        fns = [mm(psS[L][:, comp * 512:
                                                         comp * 512 + w0],
                                                  lhq,
                                                  ckp[pp][:, kbase + kc * 512:
                                                      kbase + kc * 512 + w0],
                                                  True, kc != nfull)]
                                        rd = [b_cq[qi], b_ckp[pp]]
                                        if kc == nfull:
                                            fns.append(mm(
                                                psS[L][:, comp * 512:
                                                       comp * 512 + w0],
                                                utb[:],
                                                r4b[:, R_OFF[aa]:R_OFF[aa] + w0],
                                                False, True))
                                            rd.append(b_consts)
                                        ctx.op("t", fns, reads=rd,
                                               writes=[b_psS[L][comp]])
                                        ctx.op("a", act(
                                            eX[:, kc * 512:kc * 512 + w0],
                                            psS[L][:, comp * 512:comp * 512 + w0],
                                            AF.Exp, scale=ATT_SCALE,
                                            accum=zb[L][qp][:, comp * 4 + kc:
                                                            comp * 4 + kc + 1]),
                                            reads=[b_psS[L][comp]],
                                            writes=[bX, b_z[L][qp]])

                        def comb(L, qt):
                            qp = qt % 2
                            kmax = (qt + 1) * 128
                            nkc = (qt * 128) // 512 + 1
                            zb3 = zb[L][qp][:].rearrange("p (c q) -> p c q",
                                                         q=4)[:, :, 0:nkc]
                            ctx.op("v", red(zs[L][qp][:], zb3, ALU.add),
                                   reads=[b_z[L][qp]], writes=[b_zs[L][qp]])
                            ctx.op("v", (lambda o=zr[L][qp][:, 1:2],
                                         i=zs[L][qp][:, 1:2]:
                                         (lambda e: e.reciprocal(o, i)))(),
                                   reads=[b_zs[L][qp]], writes=[b_zs[L][qp]])
                            ctx.op("v", stt(a2s[L][qp][:], zr[L][qp][:, 1:2],
                                            neglam_b[:], zs[L][qp][:, 0:1],
                                            ALU.mult, ALU.mult),
                                   reads=[b_zs[L][qp], b_small],
                                   writes=[b_zs[L][qp]])
                            ctx.op("v", stt(pbuf[L][qp][:, 0:kmax],
                                            e2[L][qp][:, 0:kmax],
                                            a2s[L][qp][:], e1[L][qp][:, 0:kmax],
                                            ALU.mult, ALU.add),
                                   reads=[b_e2[L][qp], b_e1[L][qp], b_zs[L][qp]],
                                   writes=[b_pb[L][qp]])
                            ctx.op("v", stt(ez[L][:, qt:qt + 1],
                                            zs[L][qp][:, 0:1], 256.0 * EPS,
                                            zs[L][qp][:, 0:1],
                                            ALU.mult, ALU.mult),
                                   reads=[b_zs[L][qp]], writes=[b_ez[L][qt]])

                        def tr_qt(L, qt):
                            qp = qt % 2
                            if qt % 2 == 0:
                                kbz = qt + 1
                                ctx.op("v", (lambda o=bufPT3[L][:, kbz:kbz + 1,
                                                              0:128]:
                                             (lambda e: e.memset(o, 0.0)))(),
                                       writes=[b_PT[L]])
                            qoff = (qt % 2) * 128
                            nkb = qt + 1
                            for k0 in range(0, nkb, 8):
                                kn = min(8, nkb - k0)
                                ctx.op("t",
                                       [tr(psPT[L][:, (kb - k0) * 128:
                                               (kb - k0 + 1) * 128],
                                           pbuf[L][qp][:, kb * 128:(kb + 1) * 128],
                                           id_bf[:])
                                        for kb in range(k0, k0 + kn)],
                                       reads=[b_pb[L][qp], b_consts],
                                       writes=[b_psPT[L]])
                                src2 = psPT[L][:, 0:kn * 128].rearrange(
                                    "p (n c) -> p n c", c=128)
                                dst2 = bufPT3[L][:, k0:k0 + kn, qoff:qoff + 128]
                                ctx.op("v", cp(dst2, src2),
                                       reads=[b_psPT[L]], writes=[b_PT[L]])

                        def o_mms(hp, L, qg):
                            pp = hp % 2
                            nkb_g = 2 * qg + 2
                            fns = []
                            for ec in range(2):
                                for kb in range(nkb_g):
                                    fns.append(mm(
                                        psO[L][:, ec * QG:(ec + 1) * QG],
                                        cvp[pp][:, kb * 256 + ec * 128:
                                            kb * 256 + ec * 128 + 128],
                                        bufPT[L][:, kb * QG:(kb + 1) * QG],
                                        kb == 0, kb == nkb_g - 1))
                            ctx.op("t", fns, reads=[b_cvp[pp], b_PT[L]],
                                   writes=[b_psO[L]])
                            ctx.op("a", act(osq[L][:], psO[L][:], AF.Square),
                                   reads=[b_psO[L]], writes=[b_osq[L]])
                            ctx.op("a", act(o_sb[L][:], psO[L][:], AF.Identity),
                                   reads=[b_psO[L]], writes=[b_osb[L]])

                        def o_fin(hp, L, qg):
                            h = 2 * hp + L
                            for hq in range(2):
                                qt_h = 2 * qg + hq
                                c0 = hq * 128
                                ctx.op("t", [mm(psO[L][0:1, c0:c0 + 128],
                                                onesbf[:, 0:1],
                                                osq[L][:, c0:c0 + 128],
                                                True, False),
                                             mm(psO[L][0:1, c0:c0 + 128],
                                                onesbf[:, 0:1],
                                                osq[L][:, QG + c0:QG + c0 + 128],
                                                False, False),
                                             mm(psO[L][0:1, c0:c0 + 128],
                                                ez[L][:, qt_h:qt_h + 1],
                                                id_bf[:], False, True)],
                                       reads=[b_osq[L], b_consts, b_ez[L][qt_h]],
                                       writes=[b_psO[L]])
                            ctx.op("a", act(S_sb[L][0:1, 0:QG], psO[L][0:1, 0:QG],
                                            AF.Ln, scale=1.0 / 256.0),
                                   reads=[b_psO[L]], writes=[b_Ssb[L]])
                            ctx.op("a", act(S_sb[L][0:1, 0:QG], S_sb[L][0:1, 0:QG],
                                            AF.Exp, scale=-0.5),
                                   reads=[b_Ssb[L]], writes=[b_Ssb[L]])
                            ctx.op("t", mm(psO[L][:, 0:QG], onesb[0:1, :],
                                           S_sb[L][0:1, 0:QG], True, True),
                                   reads=[b_Ssb[L]], writes=[b_psO[L]])
                            ctx.op("v", cp(S_sb[L][:], psO[L][:, 0:QG]),
                                   reads=[b_psO[L]], writes=[b_Ssb[L]])
                            for ec in range(2):
                                dst3 = actA[:, (2 * h + ec) * S + qg * QG:
                                            (2 * h + ec) * S + (qg + 1) * QG]
                                ctx.op("v", stt(dst3,
                                                o_sb[L][:, ec * QG:(ec + 1) * QG],
                                                wCb[:, ec:ec + 1], S_sb[L][:],
                                                ALU.mult, ALU.mult),
                                       reads=[b_osb[L], b_small, b_Ssb[L]],
                                       writes=[b_actA])

                        kv_load(0)
                        q_load(0)
                        q_load(1)
                        for hp in range(cfg.DH // 2):
                            if hp + 1 < cfg.DH // 2:
                                kv_load(hp + 1)
                                q_load(2 * hp + 2)
                                q_load(2 * hp + 3)
                            scores_pair(hp, 0)
                            pend = None
                            for qt in range(NT):
                                comb(0, qt)
                                comb(1, qt)
                                if qt + 1 < NT:
                                    scores_pair(hp, qt + 1)
                                if pend is not None:
                                    o_fin(hp, 0, pend)
                                    o_fin(hp, 1, pend)
                                    pend = None
                                tr_qt(0, qt)
                                tr_qt(1, qt)
                                if qt % 2 == 1:
                                    o_mms(hp, 0, qt // 2)
                                    o_mms(hp, 1, qt // 2)
                                    pend = qt // 2
                            o_fin(hp, 0, pend)
                            o_fin(hp, 1, pend)

            ctx.set_fence()
            with ExitStack() as d_es:
                dsb = lambda name, shape, dt: d_es.enter_context(
                    nc.sbuf_tensor(name + f"_d{seq}", shape, dt))
                owt = dsb("owt", [128, cfg.NE * d], BF16)
                at_tm = [dsb("atta", [128, d], BF16), dsb("attb", [128, d], BF16)]
                xt = [dsb("xta", [128, d], F32), dsb("xtb", [128, d], F32)]
                b_xt = [B_(), B_()]
                bfull = dsb("bfull", [128, d], F32)
                xm1s = dsb("xm1", [128, d], F32)
                xm1 = [xm1s, xm1s]
                xms = [dsb("xms0", [128, d], F32), dsb("xms1", [128, d], F32)]
                sp4 = [dsb("sp4a", [128, 4], F32), dsb("sp4b", [128, 4], F32)]
                jnkD = xm1
                b_owt = B_()
                b_att = [B_(), B_()]
                b_bfull = B_()
                b_xm1s = B_()
                b_xm1 = [b_xm1s, b_xm1s]
                b_xms = [B_(), B_()]
                b_jnk = b_xm1
                src = outwT.rearrange("(c p) o -> p c o", p=128)
                dma("g", "dw0", owt[:].rearrange("p (c o) -> p c o", o=d), src,
                    writes=[b_owt])
                ctx.set_fence()
                with ExitStack() as ps_es:
                    psD = [ps_es.enter_context(
                        nc.psum_tensor(f"psD{i}_{seq}", [128, 512], F32))
                        for i in range(8)]
                    b_psD = [B_() for _ in range(8)]
                    with nc.sbuf_tensor(f"obrow_{seq}", [1, d], F32) as obrow:
                        b_obr = B_()
                        dma("y", "dc0", obrow[:], outb_row, writes=[b_obr])
                        for j in range(0, d, 512):
                            wj = min(512, d - j)
                            ctx.op("t", mm(psD[0][:, 0:wj], onesb[0:1, :],
                                           obrow[0:1, j:j + wj], True, True),
                                   reads=[b_obr], writes=[b_psD[0]])
                            ctx.op("v", cp(bfull[:, j:j + wj], psD[0][:, 0:wj]),
                                   reads=[b_psD[0]], writes=[b_bfull])
                    ctx.set_fence()

                    noc4 = (d + 511) // 512

                    def d_mm(t):
                        bank0 = 4 * (t % 2)
                        dma("y", f"dx{t % 2}", xt[t % 2][:],
                            xin[seq, t * 128:(t + 1) * 128, :],
                            writes=[b_xt[t % 2]])
                        for oc4 in range(noc4):
                            w4 = min(512, d - oc4 * 512)
                            pp = psD[bank0 + oc4]
                            bpp = b_psD[bank0 + oc4]
                            fns = [mm(pp[:, 0:w4],
                                      actA[:, ec * S + t * 128:ec * S + (t + 1) * 128],
                                      owt[:, ec * d + oc4 * 512:
                                          ec * d + oc4 * 512 + w4],
                                      ec == 0, ec == cfg.NE - 1)
                                   for ec in range(cfg.NE)]
                            ctx.op("t", fns, reads=[b_owt, b_actA], writes=[bpp])

                    def d_ln(t):
                        gt = seq * NT + t
                        par = t % 2
                        bank0 = 4 * (t % 2)
                        st = stats[par]
                        bsp = b_stp[par]
                        for oc4 in range(noc4):
                            w4 = min(512, d - oc4 * 512)
                            ctx.op("v", stt(at_tm[par][:, oc4 * 512:oc4 * 512 + w4],
                                            psD[bank0 + oc4][:, 0:w4], 1.0,
                                            bfull[:, oc4 * 512:oc4 * 512 + w4],
                                            ALU.mult, ALU.add,
                                            accum=sp4[par][:, oc4:oc4 + 1]),
                                   reads=[b_psD[bank0 + oc4], b_bfull],
                                   writes=[b_att[par], bsp])
                        a_ap = at_tm[par][:]
                        ctx.op("v", red(st[0][:], sp4[par][:, 0:noc4], ALU.add),
                               reads=[bsp], writes=[bsp])
                        ctx.op("a", act(jnkD[par][:], a_ap, AF.Square,
                                        accum=st[1][:]),
                               reads=[b_att[par]], writes=[b_jnk[par], bsp])
                        _ln_smalls(ctx, d, st[0][:], st[1][:], st[2][:], st[3][:],
                                   st[4][:], epsb[:], bsp, b_eps)
                        ctx.op("v", stt(xm1[par][:], a_ap, st[3][:], xt[par][:],
                                        ALU.mult, ALU.add),
                               reads=[b_att[par], bsp, b_xt[par]],
                               writes=[b_xm1[par]])
                        ctx.op("a", act(xms[par][:], xm1[par][:], AF.Identity,
                                        bias=st[4][:], accum=st[0][:]),
                               reads=[b_xm1[par], bsp], writes=[b_xms[par], bsp])
                        ctx.op("a", act(jnkD[par][:], xms[par][:], AF.Square,
                                        accum=st[1][:]),
                               reads=[b_xms[par]], writes=[b_jnk[par], bsp])
                        ctx.op("v", red(st[5][:], xms[par][:], ALU.max),
                               reads=[b_xms[par]], writes=[bsp])
                        ctx.op("v", red(st[2][:], xms[par][:], ALU.min),
                               reads=[b_xms[par]], writes=[bsp])
                        dma("y", f"dm{par}", xmid_d[gt, :, :], xms[par][:],
                            reads=[b_xms[par]], writes=[b_xmid[gt]])
                        _ln_smalls(ctx, d, st[0][:], st[1][:], st[3][:],
                                   i2s[:, gt:gt + 1], None, epsb[:], bsp, b_eps,
                                   b_keep=b_stats, mean_ap=m2s[:, gt:gt + 1])
                        ctx.op("v", (lambda o=st[5][:], m=m2s[:, gt:gt + 1]:
                                     (lambda e: e.tensor_sub(o, o, m)))(),
                               reads=[bsp, b_stats], writes=[bsp])
                        ctx.op("v", (lambda o=st[2][:], m=m2s[:, gt:gt + 1]:
                                     (lambda e: e.tensor_sub(o, m, o)))(),
                               reads=[bsp, b_stats], writes=[bsp])
                        ctx.op("v", (lambda o=st[5][:], i=st[2][:]:
                                     (lambda e: e.tensor_max(o, o, i)))(),
                               reads=[bsp], writes=[bsp])
                        ctx.op("v", (lambda o=absm[:, gt:gt + 1], a0=st[5][:],
                                     i=i2s[:, gt:gt + 1]:
                                     (lambda e: e.tensor_mul(o, a0, i)))(),
                               reads=[bsp, b_stats], writes=[b_stats])

                    for i in range(NT + 1):
                        if i < NT:
                            d_mm(i)
                        if i >= 1:
                            d_ln(i - 1)
            actA_g.__exit__(None, None, None)

        actA_fence = list(b_actA.w) + list(b_actA.r)
        with ExitStack() as f_es:
            fsb = lambda name, shape, dt: f_es.enter_context(
                nc.sbuf_tensor(name, shape, dt))
            wf = fsb("wf", [128, nf * d], BF16)
            xtf = [fsb(f"xtf{i}", [128, d], F32) for i in range(4)]
            fbias = fsb("fbias", [128, d], F32)
            b_wf = Buf(actA_fence)
            b_xtf = [Buf(actA_fence) for _ in range(4)]
            b_fbias = Buf(actA_fence)
            ctx.set_fence()
            xq8 = [fsb(f"xq8{i}", [128, d], F8) for i in range(4)]
            xqb = [fsb(f"xqb{i}", [128, d], BF16) for i in range(4)]
            xqf = [fsb(f"xqf{i}", [128, nf * 128], BF16) for i in range(4)]
            fb = [fsb("fba", [128, d], F32), fsb("fbb", [128, d], F32)]
            ot1s = fsb("ot1", [128, d], F32)
            ot1 = [ot1s, ot1s]
            ost = [fsb("ost0", [128, d], F32), fsb("ost1", [128, d], F32)]
            sct = [fsb(f"sct{i}", [128, 1], F32) for i in range(4)]
            sp4f = [fsb("sp4fa", [128, 4], F32), fsb("sp4fb", [128, 4], F32)]
            sbt = [fsb(f"sbt{i}", [128, 1], F32) for i in range(4)]
            b_xq8 = [B_() for _ in range(4)]
            b_xqb = [B_() for _ in range(4)]
            b_xqf = [B_() for _ in range(4)]
            b_fb = [B_(), B_()]
            b_ot1s = B_()
            b_ot1 = [b_ot1s, b_ot1s]
            b_ost = [B_(), B_()]
            b_sc = [B_() for _ in range(4)]

            src = ffnT.rearrange("(c p) o -> p c o", p=128)
            dma("g", "dw1", wf[:].rearrange("p (c o) -> p c o", o=d), src,
                writes=[b_wf])

            b_cc = B_()
            with nc.psum_tensor("psX", [128, 128], F32) as psX:
                b_psX = B_()
                ctx.op("v", red(stats[0][0][:], absm[:, 0:cfg.NT_ALL], ALU.max),
                       reads=[b_stats], writes=[b_stp[0]])
                ctx.op("t", mm(psX[0:1, :], stats[0][0][:], id_f32[:], True, True),
                       reads=[b_stp[0], b_consts], writes=[b_psX])
                ctx.op("v", red(smallb[0:1, 0:1], psX[0:1, :], ALU.max),
                       reads=[b_psX], writes=[b_small])
                dma("g", "dcc0", cc_in.ap(), smallb[0:1, 0:1],
                    reads=[b_small], writes=[b_cc])
                ctx.op("g", (lambda e: e.collective_compute(
                    "AllReduce", ALU.max,
                    replica_groups=[list(range(cfg.n_cores))],
                    ins=[cc_in.ap().opt()], outs=[cc_out.ap().opt()])),
                       reads=[b_cc], writes=[b_cc])
                dma("g", "dcc0", smallb[0:1, 1:2], cc_out.ap(),
                    reads=[b_cc], writes=[b_small])
                ctx.op("v", ts(smallb[0:1, 2:3], smallb[0:1, 1:2], 1.0 / FP8_MAX,
                               SCALE_EPS, ALU.mult, ALU.max),
                       reads=[b_small], writes=[b_small])
                ctx.op("v", (lambda o=smallb[0:1, 2:3]:
                             (lambda e: e.reciprocal(o, o)))(),
                       reads=[b_small], writes=[b_small])
                ctx.op("v", ts(smallb[0:1, 2:3], smallb[0:1, 2:3], 0.5, None,
                               ALU.mult),
                       reads=[b_small], writes=[b_small])
                ctx.op("v", ts(smallb[0:1, 3:4], wssb[0:1, :], 2.0, None, ALU.mult),
                       reads=[b_consts], writes=[b_small])
                ctx.op("t", [mm(psX[:, 0:1], onesb[0:1, :], smallb[0:1, 2:3],
                                True, True),
                             mm(psX[:, 1:2], onesb[0:1, :], smallb[0:1, 3:4],
                                True, True)],
                       reads=[b_small], writes=[b_psX])
                ctx.op("v", cp(g2[:], psX[:, 0:2]), reads=[b_psX], writes=[b_g2])

            ctx.set_fence()
            with ExitStack() as ps_es:
                psF = [ps_es.enter_context(
                    nc.psum_tensor(f"psF{i}", [128, 512], F32)) for i in range(4)]
                psPT3 = [ps_es.enter_context(
                    nc.psum_tensor(f"psPT3{i}", [128, 1024], BF16))
                    for i in range(2)]
                b_psF = [B_() for _ in range(4)]
                b_psPT3 = [B_(), B_()]
                with nc.sbuf_tensor("fbrow", [1, d], F32) as fbrow:
                    b_fbrow = B_()
                    dma("y", "dc0", fbrow[:], ffnb_row, writes=[b_fbrow])
                    for j in range(0, d, 512):
                        wj = min(512, d - j)
                        ctx.op("t", mm(psF[0][:, 0:wj], onesb[0:1, :],
                                       fbrow[0:1, j:j + wj], True, True),
                               reads=[b_fbrow], writes=[b_psF[0]])
                        ctx.op("v", cp(fbias[:, j:j + wj], psF[0][:, 0:wj]),
                               reads=[b_psF[0]], writes=[b_fbias])
                ctx.set_fence()

                def f_prep_a(t):
                    p4 = t % 4
                    dma("y", f"dxf{p4}", xtf[p4][:], xmid_d[t, :, :],
                        reads=[b_xmid[t]], writes=[b_xtf[p4]])
                    ctx.op("v", (lambda o=sct[p4][:], a0=i2s[:, t:t + 1],
                                 i=g2[:, 0:1]:
                                 (lambda e: e.tensor_mul(o, a0, i)))(),
                           reads=[b_stats, b_g2], writes=[b_sc[p4]])
                    ctx.op("v", stt(sbt[p4][:], m2s[:, t:t + 1], -1.0, sct[p4][:],
                                    ALU.mult, ALU.mult),
                           reads=[b_stats, b_sc[p4]], writes=[b_sc[p4]])
                    ctx.op("a", act(xq8[p4][:], xtf[p4][:], AF.Identity,
                                    bias=sbt[p4][:], scale=sct[p4][:]),
                           reads=[b_xtf[p4], b_sc[p4]], writes=[b_xq8[p4]])
                    ctx.op("a", act(xqb[p4][:], xq8[p4][:], AF.Identity),
                           reads=[b_xq8[p4]], writes=[b_xqb[p4]])

                def f_prep_b(t):
                    p4 = t % 4
                    for k0 in range(0, nf, 4):
                        hp = (k0 // 4) % 2
                        kn = min(4, nf - k0)
                        ctx.op("t", [tr(psPT3[hp][:, (k - k0) * 128:
                                            (k - k0 + 1) * 128],
                                        xqb[p4][:, k * 128:(k + 1) * 128],
                                        id_bf[:])
                                     for k in range(k0, k0 + kn)],
                               reads=[b_xqb[p4], b_consts], writes=[b_psPT3[hp]])
                        ctx.op("v", cp(xqf[p4][:, k0 * 128:(k0 + kn) * 128],
                                       psPT3[hp][:, 0:kn * 128]),
                               reads=[b_psPT3[hp]], writes=[b_xqf[p4]])

                def f_compute(t):
                    p4 = t % 4
                    par = t % 2
                    st = stats[par]
                    bsp = b_stp[par]
                    for oc4 in range((d + 511) // 512):
                        w4 = min(512, d - oc4 * 512)
                        bk = (t % 2) * 2 + oc4 % 2
                        pp, bpp = psF[bk], b_psF[bk]
                        fns = [mm(pp[:, 0:w4], xqf[p4][:, ic * 128:(ic + 1) * 128],
                                  wf[:, ic * d + oc4 * 512:ic * d + oc4 * 512 + w4],
                                  ic == 0, ic == nf - 1) for ic in range(nf)]
                        ctx.op("t", fns, reads=[b_xqf[p4], b_wf], writes=[bpp])
                        ctx.op("v", stt(fb[par][:, oc4 * 512:oc4 * 512 + w4],
                                        pp[:, 0:w4], g2[:, 1:2],
                                        fbias[:, oc4 * 512:oc4 * 512 + w4],
                                        ALU.mult, ALU.add,
                                        accum=sp4f[par][:, oc4:oc4 + 1]),
                               reads=[bpp, b_g2, b_fbias],
                               writes=[b_fb[par], bsp])
                    ctx.op("v", red(st[0][:], sp4f[par][:, 0:(d + 511) // 512],
                                    ALU.add),
                           reads=[bsp], writes=[bsp])
                    ctx.op("a", act(xqb[p4][:], fb[par][:], AF.Square,
                                    accum=st[1][:]),
                           reads=[b_fb[par]], writes=[b_xqb[p4], bsp])
                    _ln_smalls(ctx, d, st[0][:], st[1][:], st[2][:], st[3][:],
                               st[4][:], epsb[:], bsp, b_eps)
                    ctx.op("v", stt(ot1[par][:], fb[par][:], st[3][:], xtf[p4][:],
                                    ALU.mult, ALU.add),
                           reads=[b_fb[par], bsp, b_xtf[p4]], writes=[b_ot1[par]])
                    ctx.op("a", act(ost[par][:], ot1[par][:], AF.Identity,
                                    bias=st[4][:]),
                           reads=[b_ot1[par], bsp], writes=[b_ost[par]])
                    seqi, ti = t // NT, t % NT
                    dma("y", f"do{par}", xout[seqi, ti * 128:(ti + 1) * 128, :],
                        ost[par][:], reads=[b_ost[par]], writes=[b_out])

                for i in range(cfg.NT_ALL + 2):
                    if i < cfg.NT_ALL:
                        f_prep_a(i)
                    if i >= 2:
                        f_compute(i - 2)
                    if i < cfg.NT_ALL:
                        f_prep_b(i)

        fin = []
        for sname in ("do0", "do1"):
            if sname in ctx.cnt:
                fin.append((sname, ctx.cnt[sname]))
        ctx.raw_wait("y", fin)

        handles = {}
        with ExitStack() as sem_es:
            for sname in ctx.semnames:
                handles[sname] = sem_es.enter_context(nc.semaphore(sname))
            with nc.Block() as block:
                @block.tensor
                def _(e):
                    ctx.emit(e, "t", handles)

                @block.vector
                def _(e):
                    ctx.emit(e, "v", handles)

                @block.scalar
                def _(e):
                    ctx.emit(e, "a", handles)

                @block.gpsimd
                def _(e):
                    ctx.emit(e, "g", handles)

                @block.sync
                def _(e):
                    ctx.emit(e, "y", handles)
    return nc


_BUILD_CACHE = {}


def _get_nc(cfg: Cfg):
    key = (cfg.B, cfg.S, cfg.d, cfg.H, cfg.KV, cfg.n_cores)
    if key not in _BUILD_CACHE:
        _BUILD_CACHE[key] = build(cfg)
    return _BUILD_CACHE[key]


def kernel(L0_attn_hidden_states, L0_attn_Wqkv_w, L0_attn_Wqkv_b,
           L0_attn_lambda_q1, L0_attn_lambda_k1, L0_attn_lambda_q2,
           L0_attn_lambda_k2, L0_attn_subln_w, L0_attn_out_w, L0_attn_out_b,
           L0_ffn_weight, L0_ffn_weight_scale, L0_ffn_bias):
    x = np.asarray(L0_attn_hidden_states)
    B, S, d = x.shape
    H = 16
    KV = 8
    cfg = Cfg(B=B, S=S, d=d, H=H, KV=KV, n_cores=8)
    nc = _get_nc(cfg)

    c = np.ascontiguousarray
    wqkvT = c(np.asarray(L0_attn_Wqkv_w).T)
    qkvb_t = c(np.asarray(L0_attn_Wqkv_b).reshape(cfg.OP // 128, 128).T)
    biasv_row = c(np.asarray(L0_attn_Wqkv_b)[H * 128 + KV * 128:].reshape(1, cfg.KVD))
    lam4 = c(np.stack([np.asarray(L0_attn_lambda_q1), np.asarray(L0_attn_lambda_k1),
                       np.asarray(L0_attn_lambda_q2), np.asarray(L0_attn_lambda_k2)],
                      axis=1))
    subln2 = c(np.asarray(L0_attn_subln_w).reshape(2, 128).T)
    outwT = c(np.asarray(L0_attn_out_w).T)
    outb_row = c(np.asarray(L0_attn_out_b).reshape(1, d))
    ffnT = c(np.asarray(L0_ffn_weight).T)
    ffnb_row = c(np.asarray(L0_ffn_bias).reshape(1, d))
    ws = c(np.asarray(L0_ffn_weight_scale).reshape(1, 1))

    in_maps = []
    for core in range(cfg.n_cores):
        xs = c(x[core * cfg.B_loc:(core + 1) * cfg.B_loc])
        in_maps.append({
            "xin": xs,
            "xin_fm": c(xs.transpose(0, 2, 1)),
            "qkvb_row": c(np.asarray(L0_attn_Wqkv_b).reshape(1, cfg.OP)),
            "wqkvT": wqkvT, "qkvb_t": qkvb_t, "biasv_row": biasv_row,
            "lam4": lam4, "subln2": subln2, "outwT": outwT, "outb_row": outb_row,
            "ffnT": ffnT, "ffnb_row": ffnb_row, "ws_in": ws,
        })
    res = run_bass_kernel_spmd(nc, in_maps, core_ids=list(range(cfg.n_cores)))
    return np.concatenate([r["xout"] for r in res.results], axis=0)


# version 37
# speedup vs baseline: 1.0872x; 1.0872x over previous
import math
from contextlib import ExitStack

import numpy as np
import ml_dtypes

import concourse.bass as bass
import concourse.mybir as mybir
from concourse.bass_utils import run_bass_kernel_spmd

F32 = mybir.dt.float32
BF16 = mybir.dt.bfloat16
F8 = mybir.dt.float8e4
AF = mybir.ActivationFunctionType
ALU = mybir.AluOpType
AX = mybir.AxisListType

LAMBDA_INIT = 0.8 - 0.6 * math.exp(-0.3)
ATT_SCALE = 128.0 ** -0.5
NEGBIG = -1.0e30
EPS = 1e-5
FP8_MAX = 448.0
SCALE_EPS = 1e-7


class Cfg:
    def __init__(self, B=16, S=2048, d=2048, H=16, KV=8, n_cores=8):
        self.B, self.S, self.d, self.H, self.KV, self.n_cores = B, S, d, H, KV, n_cores
        self.D = 128
        self.B_loc = B // n_cores
        self.nf = d // 128
        self.DH = H // 2
        self.G = KV // 2
        self.OP = (H + 2 * KV) * 128
        self.NQ = H
        self.NK = KV
        self.NE = H
        self.KVD = KV * 128
        self.NT = S // 128
        self.NT_ALL = self.B_loc * self.NT
        self.QG = 256
        assert self.OP % 256 == 0 and d % 256 == 0 and S % 512 == 0


class Buf:
    __slots__ = ("w", "r")

    def __init__(self, fence=None):
        self.w = list(fence) if fence else []
        self.r = []


class Ctx:
    ENGINES = ("t", "v", "a", "g", "y")

    def __init__(self):
        self.eng_ops = {e: [] for e in self.ENGINES}
        self.cnt = {}
        self.seen = {e: {} for e in self.ENGINES}
        self.semnames = []
        self.stream_last = {}
        self.fence = None

    def sem(self, name):
        if name not in self.cnt:
            self.cnt[name] = 0
            self.semnames.append(name)
        return name

    def buf(self):
        return Buf(self.fence)

    def set_fence(self):
        self.fence = [(s, v) for s, v in self.cnt.items() if v > 0]

    def _waits(self, eng, toks):
        best = {}
        for (s, v) in toks:
            if v > best.get(s, 0):
                best[s] = v
        out = []
        for s, v in best.items():
            if v > self.seen[eng].get(s, 0):
                out.append((s, v))
                self.seen[eng][s] = v
        return out

    def op(self, eng, fns, reads=(), writes=(), stream=None):
        if not isinstance(fns, (list, tuple)):
            fns = [fns]
        toks = []
        for b in reads:
            toks += b.w
        for b in writes:
            toks += b.w + b.r
        waits = self._waits(eng, toks)
        if stream is not None:
            semn = self.sem(stream)
            prev = self.stream_last.get(semn)
            if prev is not None:
                for wv in self._waits(eng, [prev]):
                    waits.append(wv)
            self.cnt[semn] += 16
            tok = (semn, self.cnt[semn])
            self.stream_last[semn] = tok
            amount = 16
        else:
            semn = self.sem("e" + eng)
            self.cnt[semn] += 1
            tok = (semn, self.cnt[semn])
            amount = 1
        self.eng_ops[eng].append((fns, waits, semn, amount))
        for b in reads:
            b.r.append(tok)
        for b in writes:
            b.w = [tok]
            b.r = []
        return tok

    def raw_wait(self, eng, toks):
        waits = self._waits(eng, toks)
        if waits:
            self.eng_ops[eng].append(([None], waits, None, 0))

    def emit(self, eng_obj, eng, handles):
        for fns, waits, semn, amount in self.eng_ops[eng]:
            fns_real = [f for f in fns if f is not None]
            if not fns_real:
                for (s, v) in waits:
                    eng_obj.wait_ge(handles[s], v)
                continue
            for (s, v) in waits[1:]:
                eng_obj.wait_ge(handles[s], v)
            for i, fn in enumerate(fns_real):
                inst = fn(eng_obj)
                if i == 0 and waits:
                    inst._wait_ge(handles[waits[0][0]], waits[0][1])
                if semn is not None and i == len(fns_real) - 1:
                    inst.then_inc(handles[semn], amount)


def _ln_smalls(ctx, d, sum_ap, ssq_ap, var_ap, inv_ap, negminv_ap, eps_ap,
               b_tmp, b_eps, b_keep=None, mean_ap=None):
    ALUl, AFl = ALU, AF
    bk = b_keep if b_keep is not None else b_tmp
    m_ap = mean_ap if mean_ap is not None else sum_ap
    ctx.op("v", (lambda o=m_ap, i=sum_ap, c=1.0 / d:
                 (lambda e: e.tensor_scalar_mul(o, i, c)))(),
           reads=[b_tmp], writes=[bk])
    ctx.op("v", (lambda o=ssq_ap, c=1.0 / d:
                 (lambda e: e.tensor_scalar_mul(o, o, c)))(),
           reads=[b_tmp], writes=[b_tmp])
    ctx.op("v", (lambda o=var_ap, m=m_ap:
                 (lambda e: e.scalar_tensor_tensor(o, m, -1.0, m,
                                                   ALUl.mult, ALUl.mult)))(),
           reads=[bk], writes=[b_tmp])
    ctx.op("v", (lambda o=var_ap, i=ssq_ap:
                 (lambda e: e.tensor_add(o, o, i)))(),
           reads=[b_tmp], writes=[b_tmp])
    ctx.op("a", (lambda o=inv_ap, i=var_ap, b=eps_ap:
                 (lambda e: e.activation(out=o, in_=i, func=AFl.Ln, bias=b)))(),
           reads=[b_tmp, b_eps], writes=[bk])
    ctx.op("a", (lambda o=inv_ap:
                 (lambda e: e.activation(out=o, in_=o, func=AFl.Exp, scale=-0.5)))(),
           reads=[bk], writes=[bk])
    if negminv_ap is not None:
        ctx.op("v", (lambda o=negminv_ap, m=m_ap, i=inv_ap:
                     (lambda e: e.scalar_tensor_tensor(o, m, -1.0, i,
                                                       ALUl.mult, ALUl.mult)))(),
               reads=[bk], writes=[b_tmp])


def build(cfg: Cfg):
    nc = bass.Bass("TRN2", target_bir_lowering=False)
    S, d, nf = cfg.S, cfg.d, cfg.nf
    NT, QG = cfg.NT, cfg.QG

    xin = nc.dram_tensor("xin", [cfg.B_loc, S, d], F32, kind="ExternalInput").ap()
    wqkvT = nc.dram_tensor("wqkvT", [d, cfg.OP], F32, kind="ExternalInput").ap()
    qkvb_t = nc.dram_tensor("qkvb_t", [128, cfg.OP // 128], F32, kind="ExternalInput").ap()
    biasv_row = nc.dram_tensor("biasv_row", [1, cfg.KVD], F32, kind="ExternalInput").ap()
    lam4 = nc.dram_tensor("lam4", [128, 4], F32, kind="ExternalInput").ap()
    subln2 = nc.dram_tensor("subln2", [128, 2], F32, kind="ExternalInput").ap()
    outwT = nc.dram_tensor("outwT", [cfg.NE * 128, d], F32, kind="ExternalInput").ap()
    outb_row = nc.dram_tensor("outb_row", [1, d], F32, kind="ExternalInput").ap()
    ffnT = nc.dram_tensor("ffnT", [d, d], F32, kind="ExternalInput").ap()
    ffnb_row = nc.dram_tensor("ffnb_row", [1, d], F32, kind="ExternalInput").ap()
    ws_in = nc.dram_tensor("ws_in", [1, 1], F32, kind="ExternalInput").ap()
    xout = nc.dram_tensor("xout", [cfg.B_loc, S, d], F32, kind="ExternalOutput").ap()

    qspill = nc.dram_tensor("qspill", [cfg.B_loc, cfg.NQ, 128, S], BF16).ap()
    kspill = nc.dram_tensor("kspill", [cfg.B_loc, cfg.NK, 128, S], BF16).ap()
    vspill = nc.dram_tensor("vspill", [cfg.B_loc, cfg.G, cfg.NT, 128, 256], BF16).ap()
    xmid_d = nc.dram_tensor("xmid_d", [cfg.NT_ALL, 128, d], F32).ap()
    cc_in = nc.dram_tensor("cc_in", [1, 1], F32)
    cc_out = nc.dram_tensor("cc_out", [1, 1], F32)

    id_bf_h = nc.inline_tensor(np.eye(128, dtype=ml_dtypes.bfloat16), "id_bf")
    id_f32_h = nc.inline_tensor(np.eye(128, dtype=np.float32), "id_f32")
    ut2 = np.zeros((128, 128), dtype=ml_dtypes.bfloat16)
    for dd in range(128):
        ut2[dd, : dd + 1] = 1.0
    ut2_h = nc.inline_tensor(ut2, "ut2")
    R_OFF = [0, 128, 384, 768]
    r4 = np.zeros((128, 1280), dtype=ml_dtypes.bfloat16)
    for a in range(4):
        for j in range(128 * a + 1, 128 * (a + 1)):
            r4[j - 128 * a - 1, R_OFF[a] + j] = NEGBIG
    r4_h = nc.inline_tensor(r4, "r4")
    ones_h = nc.inline_tensor(np.ones((128, 128), dtype=np.float32), "onesf")
    onesbf_h = nc.inline_tensor(np.ones((128, 2), dtype=ml_dtypes.bfloat16), "onesbf")

    ctx = Ctx()
    es = ExitStack()
    with es:
        sb = lambda name, shape, dt: es.enter_context(nc.sbuf_tensor(name, shape, dt))
        id_bf = sb("c_idbf", [128, 128], BF16)
        id_f32 = sb("c_idf32", [128, 128], F32)
        utb = sb("c_ut2", [128, 128], BF16)
        r4b = sb("c_r4", [128, 1280], BF16)
        onesb = sb("c_ones", [128, 128], F32)
        onesbf = sb("c_onesbf", [128, 2], BF16)
        qkvbb = sb("c_qkvb", [128, cfg.OP // 128], F32)
        lamb = sb("c_lam4", [128, 4], F32)
        sub2 = sb("c_sub2", [128, 2], F32)
        wCb = sb("c_wC", [128, 2], F32)
        epsb = sb("c_eps", [128, 1], F32)
        neglam_b = sb("c_neglam", [128, 1], F32)
        smallb = sb("c_smallb", [1, 8], F32)
        g2 = sb("c_g2", [128, 2], F32)
        wssb = sb("c_ws", [1, 1], F32)
        m2s = sb("s_mean2", [128, cfg.NT_ALL], F32)
        i2s = sb("s_inv2", [128, cfg.NT_ALL], F32)
        absm = sb("s_absm", [128, cfg.NT_ALL], F32)
        stats = [[sb(f"st{i}_{p}", [128, 1], F32) for i in range(6)]
                 for p in range(2)]

        B_ = lambda: ctx.buf()
        b_consts = B_()
        b_eps = B_()
        b_stats = B_()
        b_small = B_()
        b_stp = [B_(), B_()]
        b_qspill = B_()
        b_xmid = [B_() for _ in range(cfg.NT_ALL)]
        b_g2 = B_()
        b_out = B_()

        def dma(eng_name, stream, outap, inap, reads=(), writes=()):
            return ctx.op(eng_name,
                          (lambda o=outap, i=inap:
                           (lambda e: e.dma_start(out=o, in_=i)))(),
                          reads=reads, writes=writes, stream=stream)

        dma("y", "dc0", id_bf[:], id_bf_h.ap(), writes=[b_consts])
        dma("y", "dc0", id_f32[:], id_f32_h.ap(), writes=[b_consts])
        dma("y", "dc0", utb[:], ut2_h.ap(), writes=[b_consts])
        dma("y", "dc0", r4b[:], r4_h.ap(), writes=[b_consts])
        dma("y", "dc0", onesb[:], ones_h.ap(), writes=[b_consts])
        dma("y", "dc0", onesbf[:], onesbf_h.ap(), writes=[b_consts])
        dma("y", "dc0", qkvbb[:], qkvb_t, writes=[b_consts])
        dma("y", "dc0", lamb[:], lam4, writes=[b_consts])
        dma("y", "dc0", sub2[:], subln2, writes=[b_consts])
        dma("y", "dc0", wssb[:], ws_in, writes=[b_consts])

        ctx.op("v", lambda e: e.memset(epsb[:], EPS), writes=[b_eps])
        ctx.op("v", lambda e: e.tensor_scalar_mul(wCb[:], sub2[:], 1.0 - LAMBDA_INIT),
               reads=[b_consts], writes=[b_small])

        def mm(ps_ap, lhsT_ap, rhs_ap, start, stop):
            return (lambda p=ps_ap, l=lhsT_ap, r=rhs_ap, s1=start, s2=stop:
                    (lambda e: e.matmul(p, l, r, start=s1, stop=s2)))()

        def tr(ps_ap, in_ap, ident):
            return (lambda p=ps_ap, i=in_ap, d2=ident:
                    (lambda e: e.transpose(p, i, d2)))()

        def act(out_ap, in_ap, func, bias=0.0, scale=1.0, accum=None):
            return (lambda o=out_ap, i=in_ap, f=func, b=bias, s=scale, ac=accum:
                    (lambda e: e.activation(out=o, in_=i, func=f, bias=b, scale=s,
                                            accum_out=ac)))()

        def stt(out_ap, in0, scalar, in1, op0, op1, accum=None):
            return (lambda o=out_ap, a0=in0, sc=scalar, a1=in1, p0=op0, p1=op1, ac=accum:
                    (lambda e: e.scalar_tensor_tensor(o, a0, sc, a1, p0, p1,
                                                      accum_out=ac)))()

        def ts(out_ap, in0, s1, s2, op0, op1=ALU.bypass, accum=None):
            return (lambda o=out_ap, a0=in0, x1=s1, x2=s2, p0=op0, p1=op1, ac=accum:
                    (lambda e: e.tensor_scalar(o, a0, x1, x2, p0, p1, accum_out=ac)))()

        def red(out_ap, in_ap, op, absval=False):
            return (lambda o=out_ap, i=in_ap, p=op, av=absval:
                    (lambda e: e.tensor_reduce(o, i, AX.X, p,
                                               apply_absolute_value=av)))()

        def cp(out_ap, in_ap):
            return (lambda o=out_ap, i=in_ap: (lambda e: e.tensor_copy(o, i)))()

        with nc.psum_tensor("ps_lam", [128, 8], F32) as ps_lam:
            b_pl = B_()
            ctx.op("t", [mm(ps_lam[0:1, 0:1], lamb[:, 0:1], lamb[:, 1:2], True, True),
                         mm(ps_lam[0:1, 1:2], lamb[:, 2:3], lamb[:, 3:4], True, True)],
                   reads=[b_consts], writes=[b_pl])
            ctx.op("a", act(smallb[0:1, 0:2], ps_lam[0:1, 0:2], AF.Exp),
                   reads=[b_pl], writes=[b_small])
            ctx.op("v", (lambda e: e.tensor_sub(smallb[0:1, 2:3], smallb[0:1, 0:1],
                                                smallb[0:1, 1:2])),
                   reads=[b_small], writes=[b_small])
            ctx.op("v", ts(smallb[0:1, 3:4], smallb[0:1, 2:3], LAMBDA_INIT, -1.0,
                           ALU.add, ALU.mult),
                   reads=[b_small], writes=[b_small])
            ctx.op("t", mm(ps_lam[:, 2:3], onesb[0:1, :], smallb[0:1, 3:4], True, True),
                   reads=[b_small], writes=[b_pl])
            ctx.op("v", cp(neglam_b[:], ps_lam[:, 2:3]), reads=[b_pl], writes=[b_small])

        for seq in range(cfg.B_loc):
            ctx.set_fence()
            actA_g = nc.sbuf_tensor(f"actA_{seq}", [128, max(nf, cfg.NE) * S], BF16)
            actA = actA_g.__enter__()
            with ExitStack() as seq_es:
                ssb = lambda name, shape, dt: seq_es.enter_context(
                    nc.sbuf_tensor(name + f"_s{seq}", shape, dt))
                b_actA = B_()
                b_kspill, b_vspill = B_(), B_()
                actA3 = actA[:, 0:nf * S].rearrange("p (n c) -> p n c", c=S)

                ctx.set_fence()
                with ExitStack() as ab_es:
                    asb = lambda name, shape, dt: ab_es.enter_context(
                        nc.sbuf_tensor(name + f"_ab{seq}", shape, dt))
                    bufW = [asb("bufW0", [128, nf * 256], BF16),
                            asb("bufW1", [128, nf * 256], BF16)]
                    qst = [asb("qst0", [128, 512], BF16),
                           asb("qst1", [128, 512], BF16)]
                    vst = [asb("vst0", [128, 512], BF16),
                           asb("vst1", [128, 512], BF16)]
                    vbw = asb("vbw", [128, nf * 512], BF16)
                    b_vbw = B_()
                    tmt = [asb("tmt0", [128, d], BF16), asb("tmt1", [128, d], BF16)]
                    xt = [asb("xta", [128, d], F32), asb("xtb", [128, d], F32)]
                    biasvf = asb("biasvf", [128, cfg.KVD], BF16)
                    b_biasvf = B_()
                    b_xt = [B_(), B_()]
                    b_bufW = [B_(), B_()]
                    b_qst = [B_(), B_()]
                    b_vst = [B_(), B_()]
                    b_tmt = [B_(), B_()]

                    with nc.psum_tensor(f"psA_{seq}", [128, 1024], BF16) as psA:
                        b_psA = B_()
                        for t in range(NT):
                            par = t % 2
                            st = stats[par]
                            bsp = b_stp[par]
                            dma("y", f"dx{par}", xt[par][:],
                                xin[seq, t * 128:(t + 1) * 128, :], writes=[b_xt[par]])
                            ctx.op("a", act(tmt[par][:], xt[par][:], AF.Identity,
                                            accum=st[0][:]),
                                   reads=[b_xt[par]], writes=[b_tmt[par], bsp])
                            ctx.op("a", act(tmt[par][:], xt[par][:], AF.Square,
                                            accum=st[1][:]),
                                   reads=[b_xt[par]], writes=[b_tmt[par], bsp])
                            _ln_smalls(ctx, d, st[0][:], st[1][:], st[2][:], st[3][:],
                                       st[4][:], epsb[:], bsp, b_eps)
                            ctx.op("a", act(tmt[par][:], xt[par][:], AF.Identity,
                                            bias=st[4][:], scale=st[3][:]),
                                   reads=[b_xt[par], bsp], writes=[b_tmt[par]])
                            for k0 in range(0, nf, 8):
                                kn = min(8, nf - k0)
                                ctx.op("t",
                                       [tr(psA[:, (k - k0) * 128:(k - k0 + 1) * 128],
                                           tmt[par][:, k * 128:(k + 1) * 128], id_bf[:])
                                        for k in range(k0, k0 + kn)],
                                       reads=[b_tmt[par], b_consts], writes=[b_psA])
                                src = psA[:, 0:kn * 128].rearrange(
                                    "p (n c) -> p n c", c=128)
                                dst = actA3[:, k0:k0 + kn, t * 128:(t + 1) * 128]
                                ctx.op("v", cp(dst, src), reads=[b_psA],
                                       writes=[b_actA])

                    ctx.set_fence()
                    with ExitStack() as ps_es:
                        psB = [ps_es.enter_context(
                            nc.psum_tensor(f"psB{i}_{seq}", [128, 512], F32))
                            for i in range(8)]
                        b_psB = [B_() for _ in range(8)]
                        with nc.sbuf_tensor(f"bvrow_{seq}", [1, cfg.KVD], F32) as bvr:
                            b_bvr = B_()
                            dma("y", "dc0", bvr[:], biasv_row, writes=[b_bvr])
                            for j in range(0, cfg.KVD, 512):
                                w = min(512, cfg.KVD - j)
                                ctx.op("t", mm(psB[7][:, 0:w], onesb[0:1, :],
                                               bvr[0:1, j:j + w], True, True),
                                       reads=[b_bvr], writes=[b_psB[7]])
                                ctx.op("v", cp(biasvf[:, j:j + w], psB[7][:, 0:w]),
                                       reads=[b_psB[7]], writes=[b_biasvf])
                        ctx.set_fence()
                        nslab = cfg.OP // 256
                        nqk_slab = (cfg.NQ + cfg.NK) // 2
                        for s_i in range(nslab):
                            par = s_i % 2
                            if s_i < nqk_slab:
                                src = wqkvT[:, s_i * 256:(s_i + 1) * 256] \
                                    .rearrange("(c p) o -> p c o", p=128)
                                dst_w = bufW[par][:].rearrange(
                                    "p (c o) -> p c o", o=256)
                                dma("g", f"dw{par}", dst_w, src,
                                    writes=[b_bufW[par]])
                                for j in range(2):
                                    oc = 2 * s_i + j
                                    base = 4 * (oc % 2)
                                    grp = [psB[base + q] for q in range(4)]
                                    bgrp = [b_psB[base + q] for q in range(4)]
                                    fns = []
                                    for ic in range(nf):
                                        for t4 in range(S // 512):
                                            fns.append(mm(
                                                grp[t4][:],
                                                bufW[par][:, ic * 256 + j * 128:
                                                          ic * 256 + j * 128 + 128],
                                                actA[:, ic * S + t4 * 512:
                                                     ic * S + (t4 + 1) * 512],
                                                ic == 0, ic == nf - 1))
                                    ctx.op("t", fns, reads=[b_bufW[par], b_actA],
                                           writes=bgrp)
                                    for t4 in range(S // 512):
                                        if oc < cfg.NQ:
                                            qp = t4 % 2
                                            ctx.op("a", act(qst[qp][:], grp[t4][:],
                                                            AF.Identity,
                                                            bias=qkvbb[:, oc:oc + 1]),
                                                   reads=[bgrp[t4], b_consts],
                                                   writes=[b_qst[qp]])
                                            dma("y", f"dq{qp}",
                                                qspill[seq, oc, :,
                                                       t4 * 512:(t4 + 1) * 512],
                                                qst[qp][:], reads=[b_qst[qp]],
                                                writes=[b_qspill])
                                        else:
                                            kc = oc - cfg.NQ
                                            qp = t4 % 2
                                            ctx.op("a", act(qst[qp][:], grp[t4][:],
                                                            AF.Identity,
                                                            bias=qkvbb[:, oc:oc + 1]),
                                                   reads=[bgrp[t4], b_consts],
                                                   writes=[b_qst[qp]])
                                            dma("y", f"dq{qp}",
                                                kspill[seq, kc, :,
                                                       t4 * 512:(t4 + 1) * 512],
                                                qst[qp][:], reads=[b_qst[qp]],
                                                writes=[b_kspill])
                            else:
                                g4v = s_i - nqk_slab
                                nvs = nslab - nqk_slab
                                if g4v % 2 == 0 and g4v + 1 < nvs:
                                    srcv = wqkvT[:, s_i * 256:(s_i + 2) * 256] \
                                        .rearrange("(c p) o -> p c o", p=128)
                                    dma("g", f"dw{par}",
                                        vbw[:].rearrange("p (c o) -> p c o", o=512),
                                        srcv, writes=[b_vbw])
                                    for t in range(NT):
                                        pv = psB[t % 2]
                                        bpv = b_psB[t % 2]
                                        vp = t % 2
                                        fns = [mm(pv[:, 0:512],
                                                  actA[:, ic * S + t * 128:
                                                       ic * S + (t + 1) * 128],
                                                  vbw[:, ic * 512:(ic + 1) * 512],
                                                  ic == 0, ic == nf - 1)
                                               for ic in range(nf)]
                                        ctx.op("t", fns, reads=[b_vbw, b_actA],
                                               writes=[bpv])
                                        ctx.op("v", stt(
                                            vst[vp][:], pv[:, 0:512], 1.0,
                                            biasvf[:, g4v * 256:g4v * 256 + 512],
                                            ALU.mult, ALU.add),
                                            reads=[bpv, b_biasvf],
                                            writes=[b_vst[vp]])
                                        vdst = vspill[seq, g4v:g4v + 2, t, :, :] \
                                            .rearrange("g p v -> p g v")
                                        dma("y", f"dq{vp}", vdst,
                                            vst[vp][:].rearrange(
                                                "p (g v) -> p g v", v=256),
                                            reads=[b_vst[vp]],
                                            writes=[b_vspill])
                                elif g4v % 2 == 0:
                                    srcv = wqkvT[:, s_i * 256:(s_i + 1) * 256] \
                                        .rearrange("(c p) o -> p c o", p=128)
                                    dst_w = bufW[par][:].rearrange(
                                        "p (c o) -> p c o", o=256)
                                    dma("g", f"dw{par}", dst_w, srcv,
                                        writes=[b_bufW[par]])
                                    for t in range(NT):
                                        pv = psB[t % 2]
                                        bpv = b_psB[t % 2]
                                        vp = t % 2
                                        fns = [mm(pv[:, 0:256],
                                                  actA[:, ic * S + t * 128:
                                                       ic * S + (t + 1) * 128],
                                                  bufW[par][:, ic * 256:
                                                            (ic + 1) * 256],
                                                  ic == 0, ic == nf - 1)
                                               for ic in range(nf)]
                                        ctx.op("t", fns,
                                               reads=[b_bufW[par], b_actA],
                                               writes=[bpv])
                                        ctx.op("v", stt(
                                            vst[vp][:, 0:256], pv[:, 0:256], 1.0,
                                            biasvf[:, g4v * 256:g4v * 256 + 256],
                                            ALU.mult, ALU.add),
                                            reads=[bpv, b_biasvf],
                                            writes=[b_vst[vp]])
                                        dma("y", f"dq{vp}",
                                            vspill[seq, g4v, t, :, :],
                                            vst[vp][:, 0:256],
                                            reads=[b_vst[vp]],
                                            writes=[b_vspill])

                ctx.set_fence()
                with ExitStack() as c_es:
                    csb = lambda name, shape, dt: c_es.enter_context(
                        nc.sbuf_tensor(name + f"_c{seq}", shape, dt))
                    ckp = [csb("ck0", [128, 2 * S], BF16),
                           csb("ck1", [128, 2 * S], BF16)]
                    cvp = [csb("cv0", [128, NT * 256], BF16),
                           csb("cv1", [128, NT * 256], BF16)]
                    cq = [csb(f"cq{i}", [128, 2 * S], BF16) for i in range(4)]
                    e1 = [[csb(f"e1_{L}{q}", [128, S], BF16) for q in range(2)]
                          for L in range(2)]
                    e2 = [[csb(f"e2_{L}{q}", [128, S], BF16) for q in range(2)]
                          for L in range(2)]
                    pbuf = [[csb(f"pb_{L}{q}", [128, S], BF16) for q in range(2)]
                            for L in range(2)]
                    bufPT = [csb("bufPT0", [128, NT * QG], BF16),
                             csb("bufPT1", [128, NT * QG], BF16)]
                    osq = [csb("osq0", [128, 2 * QG], BF16),
                           csb("osq1", [128, 2 * QG], BF16)]
                    o_sb = [csb("osb0", [128, 2 * QG], BF16),
                            csb("osb1", [128, 2 * QG], BF16)]
                    S_sb = [csb("S_sb0", [128, QG], F32),
                            csb("S_sb1", [128, QG], F32)]
                    zb = [[csb(f"zb{L}{q}", [128, 8], F32) for q in range(2)]
                          for L in range(2)]
                    zs = [[csb(f"zs{L}{q}", [128, 2], F32) for q in range(2)]
                          for L in range(2)]
                    a2s = [[csb(f"a2{L}{q}", [128, 1], F32) for q in range(2)]
                           for L in range(2)]
                    b_ckp = [B_(), B_()]
                    b_cvp = [B_(), B_()]
                    b_cq = [B_() for _ in range(4)]
                    b_e1 = [[B_(), B_()], [B_(), B_()]]
                    b_e2 = [[B_(), B_()], [B_(), B_()]]
                    b_pb = [[B_(), B_()], [B_(), B_()]]
                    b_PT = [B_(), B_()]
                    b_osq = [B_(), B_()]
                    b_osb = [B_(), B_()]
                    b_Ssb = [B_(), B_()]
                    b_z = [[B_(), B_()], [B_(), B_()]]
                    b_zs = [[B_(), B_()], [B_(), B_()]]
                    bufPT3 = [bufPT[i][:].rearrange("p (n c) -> p n c", c=QG)
                              for i in range(2)]

                    with ExitStack() as ps_es:
                        psS = [ps_es.enter_context(
                            nc.psum_tensor(f"psS{i}_{seq}", [128, 1024], F32))
                            for i in range(2)]
                        psO = [ps_es.enter_context(
                            nc.psum_tensor(f"psO{i}_{seq}", [128, 2 * QG], F32))
                            for i in range(2)]
                        psPT = [ps_es.enter_context(
                            nc.psum_tensor(f"psPT{i}_{seq}", [128, 1024], BF16))
                            for i in range(2)]
                        b_psS = [[B_(), B_()], [B_(), B_()]]
                        b_psO = [B_(), B_()]
                        b_psPT = [B_(), B_()]

                        def kv_load(hp):
                            pp = hp % 2
                            ksrc = kspill[seq, 2 * hp:2 * hp + 2, :, :] \
                                .rearrange("c p t -> p c t")
                            dma("y", "dql0",
                                ckp[pp][:].rearrange("p (c t) -> p c t", t=S),
                                ksrc, reads=[b_kspill], writes=[b_ckp[pp]])
                            vsrc = vspill[seq, hp, :, :, :] \
                                .rearrange("t p v -> p t v")
                            dma("y", "dql1",
                                cvp[pp][:].rearrange("p (t v) -> p t v", v=256),
                                vsrc, reads=[b_vspill], writes=[b_cvp[pp]])

                        def q_load(h):
                            qi = (h % 2) * 2 + (h // 2) % 2
                            src = qspill[seq, 2 * h:2 * h + 2, :, :] \
                                .rearrange("c p t -> p c t")
                            dma("y", "dql2",
                                cq[qi][:].rearrange("p (c t) -> p c t", t=S),
                                src, reads=[b_qspill], writes=[b_cq[qi]])

                        def scores_pair(hp, qt):
                            pp = hp % 2
                            kmax = (qt + 1) * 128
                            nfull = (qt * 128) // 512
                            nkc = nfull + 1
                            aa = qt % 4
                            wrem = kmax - nfull * 512
                            qp = qt % 2
                            for kc in range(nkc):
                                w0 = 512 if kc < nfull else wrem
                                for L in range(2):
                                    qi = L * 2 + hp % 2
                                    for comp in range(2):
                                        eX = e1[L][qp] if comp == 0 else e2[L][qp]
                                        bX = (b_e1[L][qp] if comp == 0
                                              else b_e2[L][qp])
                                        lhq = cq[qi][:, comp * S + qt * 128:
                                                     comp * S + (qt + 1) * 128]
                                        kbase = comp * S
                                        fns = [mm(psS[L][:, comp * 512:
                                                         comp * 512 + w0],
                                                  lhq,
                                                  ckp[pp][:, kbase + kc * 512:
                                                      kbase + kc * 512 + w0],
                                                  True, kc != nfull)]
                                        rd = [b_cq[qi], b_ckp[pp]]
                                        if kc == nfull:
                                            fns.append(mm(
                                                psS[L][:, comp * 512:
                                                       comp * 512 + w0],
                                                utb[:],
                                                r4b[:, R_OFF[aa]:R_OFF[aa] + w0],
                                                False, True))
                                            rd.append(b_consts)
                                        ctx.op("t", fns, reads=rd,
                                               writes=[b_psS[L][comp]])
                                        ctx.op("a", act(
                                            eX[:, kc * 512:kc * 512 + w0],
                                            psS[L][:, comp * 512:comp * 512 + w0],
                                            AF.Exp, scale=ATT_SCALE,
                                            accum=zb[L][qp][:, comp * 4 + kc:
                                                            comp * 4 + kc + 1]),
                                            reads=[b_psS[L][comp]],
                                            writes=[bX, b_z[L][qp]])

                        def comb(L, qt):
                            qp = qt % 2
                            kmax = (qt + 1) * 128
                            nkc = (qt * 128) // 512 + 1
                            zb3 = zb[L][qp][:].rearrange("p (c q) -> p c q",
                                                         q=4)[:, :, 0:nkc]
                            ctx.op("v", red(zs[L][qp][:], zb3, ALU.add),
                                   reads=[b_z[L][qp]], writes=[b_zs[L][qp]])
                            ctx.op("v", (lambda o=zs[L][qp][:]:
                                         (lambda e: e.reciprocal(o, o)))(),
                                   reads=[b_zs[L][qp]], writes=[b_zs[L][qp]])
                            ctx.op("v", (lambda o=a2s[L][qp][:],
                                         i0=zs[L][qp][:, 1:2],
                                         i1=neglam_b[:]:
                                         (lambda e: e.tensor_mul(o, i0, i1)))(),
                                   reads=[b_zs[L][qp], b_small],
                                   writes=[b_zs[L][qp]])
                            ctx.op("v", ts(pbuf[L][qp][:, 0:kmax],
                                           e1[L][qp][:, 0:kmax],
                                           zs[L][qp][:, 0:1], None, ALU.mult),
                                   reads=[b_e1[L][qp], b_zs[L][qp]],
                                   writes=[b_pb[L][qp]])
                            ctx.op("v", stt(pbuf[L][qp][:, 0:kmax],
                                            e2[L][qp][:, 0:kmax],
                                            a2s[L][qp][:], pbuf[L][qp][:, 0:kmax],
                                            ALU.mult, ALU.add),
                                   reads=[b_e2[L][qp], b_zs[L][qp], b_pb[L][qp]],
                                   writes=[b_pb[L][qp]])

                        def tr_qt(L, qt):
                            qp = qt % 2
                            if qt % 2 == 0:
                                kbz = qt + 1
                                ctx.op("v", (lambda o=bufPT3[L][:, kbz:kbz + 1,
                                                              0:128]:
                                             (lambda e: e.memset(o, 0.0)))(),
                                       writes=[b_PT[L]])
                            qoff = (qt % 2) * 128
                            nkb = qt + 1
                            for k0 in range(0, nkb, 8):
                                kn = min(8, nkb - k0)
                                ctx.op("t",
                                       [tr(psPT[L][:, (kb - k0) * 128:
                                               (kb - k0 + 1) * 128],
                                           pbuf[L][qp][:, kb * 128:(kb + 1) * 128],
                                           id_bf[:])
                                        for kb in range(k0, k0 + kn)],
                                       reads=[b_pb[L][qp], b_consts],
                                       writes=[b_psPT[L]])
                                src2 = psPT[L][:, 0:kn * 128].rearrange(
                                    "p (n c) -> p n c", c=128)
                                dst2 = bufPT3[L][:, k0:k0 + kn, qoff:qoff + 128]
                                ctx.op("v", cp(dst2, src2),
                                       reads=[b_psPT[L]], writes=[b_PT[L]])

                        def o_mms(hp, L, qg):
                            pp = hp % 2
                            nkb_g = 2 * qg + 2
                            fns = []
                            for ec in range(2):
                                for kb in range(nkb_g):
                                    fns.append(mm(
                                        psO[L][:, ec * QG:(ec + 1) * QG],
                                        cvp[pp][:, kb * 256 + ec * 128:
                                            kb * 256 + ec * 128 + 128],
                                        bufPT[L][:, kb * QG:(kb + 1) * QG],
                                        kb == 0, kb == nkb_g - 1))
                            ctx.op("t", fns, reads=[b_cvp[pp], b_PT[L]],
                                   writes=[b_psO[L]])
                            ctx.op("a", act(osq[L][:], psO[L][:], AF.Square),
                                   reads=[b_psO[L]], writes=[b_osq[L]])
                            ctx.op("a", act(o_sb[L][:], psO[L][:], AF.Identity),
                                   reads=[b_psO[L]], writes=[b_osb[L]])

                        def o_fin(hp, L, qg):
                            h = 2 * hp + L
                            ctx.op("t", [mm(psO[L][0:1, 0:QG], onesbf[:, 0:1],
                                            osq[L][:, 0:QG], True, False),
                                         mm(psO[L][0:1, 0:QG], onesbf[:, 0:1],
                                            osq[L][:, QG:2 * QG], False, True)],
                                   reads=[b_osq[L], b_consts], writes=[b_psO[L]])
                            ctx.op("a", act(S_sb[L][0:1, 0:QG], psO[L][0:1, 0:QG],
                                            AF.Ln, bias=epsb[0:1, :],
                                            scale=1.0 / 256.0),
                                   reads=[b_psO[L], b_eps], writes=[b_Ssb[L]])
                            ctx.op("a", act(S_sb[L][0:1, 0:QG], S_sb[L][0:1, 0:QG],
                                            AF.Exp, scale=-0.5),
                                   reads=[b_Ssb[L]], writes=[b_Ssb[L]])
                            ctx.op("t", mm(psO[L][:, 0:QG], onesb[0:1, :],
                                           S_sb[L][0:1, 0:QG], True, True),
                                   reads=[b_Ssb[L]], writes=[b_psO[L]])
                            ctx.op("v", cp(S_sb[L][:], psO[L][:, 0:QG]),
                                   reads=[b_psO[L]], writes=[b_Ssb[L]])
                            for ec in range(2):
                                dst3 = actA[:, (2 * h + ec) * S + qg * QG:
                                            (2 * h + ec) * S + (qg + 1) * QG]
                                ctx.op("v", stt(dst3,
                                                o_sb[L][:, ec * QG:(ec + 1) * QG],
                                                wCb[:, ec:ec + 1], S_sb[L][:],
                                                ALU.mult, ALU.mult),
                                       reads=[b_osb[L], b_small, b_Ssb[L]],
                                       writes=[b_actA])

                        kv_load(0)
                        q_load(0)
                        q_load(1)
                        for hp in range(cfg.DH // 2):
                            if hp + 1 < cfg.DH // 2:
                                kv_load(hp + 1)
                                q_load(2 * hp + 2)
                                q_load(2 * hp + 3)
                            scores_pair(hp, 0)
                            pend = None
                            for qt in range(NT):
                                comb(0, qt)
                                comb(1, qt)
                                if qt + 1 < NT:
                                    scores_pair(hp, qt + 1)
                                if pend is not None:
                                    o_fin(hp, 0, pend)
                                    o_fin(hp, 1, pend)
                                    pend = None
                                tr_qt(0, qt)
                                tr_qt(1, qt)
                                if qt % 2 == 1:
                                    o_mms(hp, 0, qt // 2)
                                    o_mms(hp, 1, qt // 2)
                                    pend = qt // 2
                            o_fin(hp, 0, pend)
                            o_fin(hp, 1, pend)

            ctx.set_fence()
            with ExitStack() as d_es:
                dsb = lambda name, shape, dt: d_es.enter_context(
                    nc.sbuf_tensor(name + f"_d{seq}", shape, dt))
                noc4 = (d + 511) // 512
                ow4 = min(512, d)
                owt = [dsb(f"owt{i}", [128, cfg.NE * ow4], BF16)
                       for i in range(noc4)]
                at_tm = [dsb("atta", [128, d], BF16), dsb("attb", [128, d], BF16)]
                xt = [dsb("xta", [128, d], F32), dsb("xtb", [128, d], F32)]
                b_xt = [B_(), B_()]
                bfull = dsb("bfull", [128, d], F32)
                xm1s = dsb("xm1", [128, d], F32)
                xm1 = [xm1s, xm1s]
                xms = [dsb("xms0", [128, d], F32), dsb("xms1", [128, d], F32)]
                sp4 = [dsb("sp4a", [128, 4], F32), dsb("sp4b", [128, 4], F32)]
                jnkD = xm1
                b_owt = [B_() for _ in range(noc4)]
                b_att = [B_(), B_()]
                b_bfull = B_()
                b_xm1s = B_()
                b_xm1 = [b_xm1s, b_xm1s]
                b_xms = [B_(), B_()]
                b_jnk = b_xm1
                for i4 in range(noc4):
                    w4s = min(512, d - i4 * 512)
                    srcs = outwT[:, i4 * 512:i4 * 512 + w4s] \
                        .rearrange("(c p) o -> p c o", p=128)
                    dma("g", f"dw{i4 % 2}",
                        owt[i4][:, 0:cfg.NE * w4s].rearrange(
                            "p (c o) -> p c o", o=w4s),
                        srcs, writes=[b_owt[i4]])
                ctx.set_fence()
                with ExitStack() as ps_es:
                    psD = [ps_es.enter_context(
                        nc.psum_tensor(f"psD{i}_{seq}", [128, 512], F32))
                        for i in range(8)]
                    b_psD = [B_() for _ in range(8)]
                    with nc.sbuf_tensor(f"obrow_{seq}", [1, d], F32) as obrow:
                        b_obr = B_()
                        dma("y", "dc0", obrow[:], outb_row, writes=[b_obr])
                        for j in range(0, d, 512):
                            wj = min(512, d - j)
                            ctx.op("t", mm(psD[0][:, 0:wj], onesb[0:1, :],
                                           obrow[0:1, j:j + wj], True, True),
                                   reads=[b_obr], writes=[b_psD[0]])
                            ctx.op("v", cp(bfull[:, j:j + wj], psD[0][:, 0:wj]),
                                   reads=[b_psD[0]], writes=[b_bfull])
                    ctx.set_fence()

                    noc4 = (d + 511) // 512

                    def d_mm(t):
                        bank0 = 4 * (t % 2)
                        for oc4 in range(noc4):
                            w4 = min(512, d - oc4 * 512)
                            pp = psD[bank0 + oc4]
                            bpp = b_psD[bank0 + oc4]
                            fns = [mm(pp[:, 0:w4],
                                      actA[:, ec * S + t * 128:ec * S + (t + 1) * 128],
                                      owt[oc4][:, ec * w4:ec * w4 + w4],
                                      ec == 0, ec == cfg.NE - 1)
                                   for ec in range(cfg.NE)]
                            ctx.op("t", fns, reads=[b_owt[oc4], b_actA],
                                   writes=[bpp])

                    def d_ln(t):
                        gt = seq * NT + t
                        par = t % 2
                        bank0 = 4 * (t % 2)
                        st = stats[par]
                        bsp = b_stp[par]
                        for oc4 in range(noc4):
                            w4 = min(512, d - oc4 * 512)
                            ctx.op("v", stt(at_tm[par][:, oc4 * 512:oc4 * 512 + w4],
                                            psD[bank0 + oc4][:, 0:w4], 1.0,
                                            bfull[:, oc4 * 512:oc4 * 512 + w4],
                                            ALU.mult, ALU.add,
                                            accum=sp4[par][:, oc4:oc4 + 1]),
                                   reads=[b_psD[bank0 + oc4], b_bfull],
                                   writes=[b_att[par], bsp])
                        dma("y", f"dx{par}", xt[par][:],
                            xin[seq, t * 128:(t + 1) * 128, :], writes=[b_xt[par]])
                        a_ap = at_tm[par][:]
                        ctx.op("v", red(st[0][:], sp4[par][:, 0:noc4], ALU.add),
                               reads=[bsp], writes=[bsp])
                        ctx.op("a", act(jnkD[par][:], a_ap, AF.Square,
                                        accum=st[1][:]),
                               reads=[b_att[par]], writes=[b_jnk[par], bsp])
                        _ln_smalls(ctx, d, st[0][:], st[1][:], st[2][:], st[3][:],
                                   st[4][:], epsb[:], bsp, b_eps)
                        ctx.op("v", stt(xm1[par][:], a_ap, st[3][:], xt[par][:],
                                        ALU.mult, ALU.add),
                               reads=[b_att[par], bsp, b_xt[par]],
                               writes=[b_xm1[par]])
                        ctx.op("a", act(xms[par][:], xm1[par][:], AF.Identity,
                                        bias=st[4][:], accum=st[0][:]),
                               reads=[b_xm1[par], bsp], writes=[b_xms[par], bsp])
                        ctx.op("a", act(jnkD[par][:], xms[par][:], AF.Square,
                                        accum=st[1][:]),
                               reads=[b_xms[par]], writes=[b_jnk[par], bsp])
                        ctx.op("v", red(st[5][:], xms[par][:], ALU.max),
                               reads=[b_xms[par]], writes=[bsp])
                        ctx.op("v", red(st[2][:], xms[par][:], ALU.min),
                               reads=[b_xms[par]], writes=[bsp])
                        dma("y", f"dm{par}", xmid_d[gt, :, :], xms[par][:],
                            reads=[b_xms[par]], writes=[b_xmid[gt]])
                        _ln_smalls(ctx, d, st[0][:], st[1][:], st[3][:],
                                   i2s[:, gt:gt + 1], None, epsb[:], bsp, b_eps,
                                   b_keep=b_stats, mean_ap=m2s[:, gt:gt + 1])
                        ctx.op("v", (lambda o=st[5][:], m=m2s[:, gt:gt + 1]:
                                     (lambda e: e.tensor_sub(o, o, m)))(),
                               reads=[bsp, b_stats], writes=[bsp])
                        ctx.op("v", (lambda o=st[2][:], m=m2s[:, gt:gt + 1]:
                                     (lambda e: e.tensor_sub(o, m, o)))(),
                               reads=[bsp, b_stats], writes=[bsp])
                        ctx.op("v", (lambda o=st[5][:], i=st[2][:]:
                                     (lambda e: e.tensor_max(o, o, i)))(),
                               reads=[bsp], writes=[bsp])
                        ctx.op("v", (lambda o=absm[:, gt:gt + 1], a0=st[5][:],
                                     i=i2s[:, gt:gt + 1]:
                                     (lambda e: e.tensor_mul(o, a0, i)))(),
                               reads=[bsp, b_stats], writes=[b_stats])

                    for i in range(NT + 1):
                        if i < NT:
                            d_mm(i)
                        if i >= 1:
                            d_ln(i - 1)
            actA_g.__exit__(None, None, None)

        actA_fence = list(b_actA.w) + list(b_actA.r)
        with ExitStack() as f_es:
            fsb = lambda name, shape, dt: f_es.enter_context(
                nc.sbuf_tensor(name, shape, dt))
            wf = fsb("wf", [128, nf * d], BF16)
            xtf = [fsb(f"xtf{i}", [128, d], F32) for i in range(4)]
            fbias = fsb("fbias", [128, d], F32)
            b_wf = Buf(actA_fence)
            b_xtf = [Buf(actA_fence) for _ in range(4)]
            b_fbias = Buf(actA_fence)
            ctx.set_fence()
            xq8 = [fsb(f"xq8{i}", [128, d], F8) for i in range(4)]
            xqb = [fsb(f"xqb{i}", [128, d], BF16) for i in range(4)]
            xqf = [fsb(f"xqf{i}", [128, nf * 128], BF16) for i in range(4)]
            fb = [fsb("fba", [128, d], F32), fsb("fbb", [128, d], F32)]
            ot1s = fsb("ot1", [128, d], F32)
            ot1 = [ot1s, ot1s]
            ost = [fsb("ost0", [128, d], F32), fsb("ost1", [128, d], F32)]
            sct = [fsb(f"sct{i}", [128, 1], F32) for i in range(4)]
            sp4f = [fsb("sp4fa", [128, 4], F32), fsb("sp4fb", [128, 4], F32)]
            sbt = [fsb(f"sbt{i}", [128, 1], F32) for i in range(4)]
            b_xq8 = [B_() for _ in range(4)]
            b_xqb = [B_() for _ in range(4)]
            b_xqf = [B_() for _ in range(4)]
            b_fb = [B_(), B_()]
            b_ot1s = B_()
            b_ot1 = [b_ot1s, b_ot1s]
            b_ost = [B_(), B_()]
            b_sc = [B_() for _ in range(4)]

            src = ffnT.rearrange("(c p) o -> p c o", p=128)
            dma("g", "dw1", wf[:].rearrange("p (c o) -> p c o", o=d), src,
                writes=[b_wf])

            b_cc = B_()
            with nc.psum_tensor("psX", [128, 128], F32) as psX:
                b_psX = B_()
                ctx.op("v", red(stats[0][0][:], absm[:, 0:cfg.NT_ALL], ALU.max),
                       reads=[b_stats], writes=[b_stp[0]])
                ctx.op("t", mm(psX[0:1, :], stats[0][0][:], id_f32[:], True, True),
                       reads=[b_stp[0], b_consts], writes=[b_psX])
                ctx.op("v", red(smallb[0:1, 0:1], psX[0:1, :], ALU.max),
                       reads=[b_psX], writes=[b_small])
                dma("g", "dcc0", cc_in.ap(), smallb[0:1, 0:1],
                    reads=[b_small], writes=[b_cc])
                ctx.op("g", (lambda e: e.collective_compute(
                    "AllReduce", ALU.max,
                    replica_groups=[list(range(cfg.n_cores))],
                    ins=[cc_in.ap().opt()], outs=[cc_out.ap().opt()])),
                       reads=[b_cc], writes=[b_cc])
                dma("g", "dcc0", smallb[0:1, 1:2], cc_out.ap(),
                    reads=[b_cc], writes=[b_small])
                ctx.op("v", ts(smallb[0:1, 2:3], smallb[0:1, 1:2], 1.0 / FP8_MAX,
                               SCALE_EPS, ALU.mult, ALU.max),
                       reads=[b_small], writes=[b_small])
                ctx.op("v", (lambda o=smallb[0:1, 2:3]:
                             (lambda e: e.reciprocal(o, o)))(),
                       reads=[b_small], writes=[b_small])
                ctx.op("v", ts(smallb[0:1, 2:3], smallb[0:1, 2:3], 0.5, None,
                               ALU.mult),
                       reads=[b_small], writes=[b_small])
                ctx.op("v", ts(smallb[0:1, 3:4], wssb[0:1, :], 2.0, None, ALU.mult),
                       reads=[b_consts], writes=[b_small])
                ctx.op("t", [mm(psX[:, 0:1], onesb[0:1, :], smallb[0:1, 2:3],
                                True, True),
                             mm(psX[:, 1:2], onesb[0:1, :], smallb[0:1, 3:4],
                                True, True)],
                       reads=[b_small], writes=[b_psX])
                ctx.op("v", cp(g2[:], psX[:, 0:2]), reads=[b_psX], writes=[b_g2])

            ctx.set_fence()
            with ExitStack() as ps_es:
                psF = [ps_es.enter_context(
                    nc.psum_tensor(f"psF{i}", [128, 512], F32)) for i in range(4)]
                psPT3 = [ps_es.enter_context(
                    nc.psum_tensor(f"psPT3{i}", [128, 1024], BF16))
                    for i in range(2)]
                b_psF = [B_() for _ in range(4)]
                b_psPT3 = [B_(), B_()]
                with nc.sbuf_tensor("fbrow", [1, d], F32) as fbrow:
                    b_fbrow = B_()
                    dma("y", "dc0", fbrow[:], ffnb_row, writes=[b_fbrow])
                    for j in range(0, d, 512):
                        wj = min(512, d - j)
                        ctx.op("t", mm(psF[0][:, 0:wj], onesb[0:1, :],
                                       fbrow[0:1, j:j + wj], True, True),
                               reads=[b_fbrow], writes=[b_psF[0]])
                        ctx.op("v", cp(fbias[:, j:j + wj], psF[0][:, 0:wj]),
                               reads=[b_psF[0]], writes=[b_fbias])
                ctx.set_fence()

                def f_prep_a(t):
                    p4 = t % 4
                    dma("y", f"dxf{p4}", xtf[p4][:], xmid_d[t, :, :],
                        reads=[b_xmid[t]], writes=[b_xtf[p4]])
                    ctx.op("v", (lambda o=sct[p4][:], a0=i2s[:, t:t + 1],
                                 i=g2[:, 0:1]:
                                 (lambda e: e.tensor_mul(o, a0, i)))(),
                           reads=[b_stats, b_g2], writes=[b_sc[p4]])
                    ctx.op("v", stt(sbt[p4][:], m2s[:, t:t + 1], -1.0, sct[p4][:],
                                    ALU.mult, ALU.mult),
                           reads=[b_stats, b_sc[p4]], writes=[b_sc[p4]])
                    ctx.op("a", act(xq8[p4][:], xtf[p4][:], AF.Identity,
                                    bias=sbt[p4][:], scale=sct[p4][:]),
                           reads=[b_xtf[p4], b_sc[p4]], writes=[b_xq8[p4]])
                    ctx.op("a", act(xqb[p4][:], xq8[p4][:], AF.Identity),
                           reads=[b_xq8[p4]], writes=[b_xqb[p4]])

                def f_prep_b(t):
                    p4 = t % 4
                    for k0 in range(0, nf, 4):
                        hp = (k0 // 4) % 2
                        kn = min(4, nf - k0)
                        ctx.op("t", [tr(psPT3[hp][:, (k - k0) * 128:
                                            (k - k0 + 1) * 128],
                                        xqb[p4][:, k * 128:(k + 1) * 128],
                                        id_bf[:])
                                     for k in range(k0, k0 + kn)],
                               reads=[b_xqb[p4], b_consts], writes=[b_psPT3[hp]])
                        ctx.op("v", cp(xqf[p4][:, k0 * 128:(k0 + kn) * 128],
                                       psPT3[hp][:, 0:kn * 128]),
                               reads=[b_psPT3[hp]], writes=[b_xqf[p4]])

                def f_compute(t):
                    p4 = t % 4
                    par = t % 2
                    st = stats[par]
                    bsp = b_stp[par]
                    for oc4 in range((d + 511) // 512):
                        w4 = min(512, d - oc4 * 512)
                        bk = (t % 2) * 2 + oc4 % 2
                        pp, bpp = psF[bk], b_psF[bk]
                        fns = [mm(pp[:, 0:w4], xqf[p4][:, ic * 128:(ic + 1) * 128],
                                  wf[:, ic * d + oc4 * 512:ic * d + oc4 * 512 + w4],
                                  ic == 0, ic == nf - 1) for ic in range(nf)]
                        ctx.op("t", fns, reads=[b_xqf[p4], b_wf], writes=[bpp])
                        ctx.op("v", stt(fb[par][:, oc4 * 512:oc4 * 512 + w4],
                                        pp[:, 0:w4], g2[:, 1:2],
                                        fbias[:, oc4 * 512:oc4 * 512 + w4],
                                        ALU.mult, ALU.add,
                                        accum=sp4f[par][:, oc4:oc4 + 1]),
                               reads=[bpp, b_g2, b_fbias],
                               writes=[b_fb[par], bsp])
                    ctx.op("v", red(st[0][:], sp4f[par][:, 0:(d + 511) // 512],
                                    ALU.add),
                           reads=[bsp], writes=[bsp])
                    ctx.op("a", act(xqb[p4][:], fb[par][:], AF.Square,
                                    accum=st[1][:]),
                           reads=[b_fb[par]], writes=[b_xqb[p4], bsp])
                    _ln_smalls(ctx, d, st[0][:], st[1][:], st[2][:], st[3][:],
                               st[4][:], epsb[:], bsp, b_eps)
                    ctx.op("v", stt(ot1[par][:], fb[par][:], st[3][:], xtf[p4][:],
                                    ALU.mult, ALU.add),
                           reads=[b_fb[par], bsp, b_xtf[p4]], writes=[b_ot1[par]])
                    ctx.op("a", act(ost[par][:], ot1[par][:], AF.Identity,
                                    bias=st[4][:]),
                           reads=[b_ot1[par], bsp], writes=[b_ost[par]])
                    seqi, ti = t // NT, t % NT
                    dma("y", f"do{par}", xout[seqi, ti * 128:(ti + 1) * 128, :],
                        ost[par][:], reads=[b_ost[par]], writes=[b_out])

                for i in range(cfg.NT_ALL + 2):
                    if i < cfg.NT_ALL:
                        f_prep_a(i)
                    if i >= 2:
                        f_compute(i - 2)
                    if i < cfg.NT_ALL:
                        f_prep_b(i)

        fin = []
        for sname in ("do0", "do1"):
            if sname in ctx.cnt:
                fin.append((sname, ctx.cnt[sname]))
        ctx.raw_wait("y", fin)

        handles = {}
        with ExitStack() as sem_es:
            for sname in ctx.semnames:
                handles[sname] = sem_es.enter_context(nc.semaphore(sname))
            with nc.Block() as block:
                @block.tensor
                def _(e):
                    ctx.emit(e, "t", handles)

                @block.vector
                def _(e):
                    ctx.emit(e, "v", handles)

                @block.scalar
                def _(e):
                    ctx.emit(e, "a", handles)

                @block.gpsimd
                def _(e):
                    ctx.emit(e, "g", handles)

                @block.sync
                def _(e):
                    ctx.emit(e, "y", handles)
    return nc


_BUILD_CACHE = {}


def _get_nc(cfg: Cfg):
    key = (cfg.B, cfg.S, cfg.d, cfg.H, cfg.KV, cfg.n_cores)
    if key not in _BUILD_CACHE:
        _BUILD_CACHE[key] = build(cfg)
    return _BUILD_CACHE[key]


def kernel(L0_attn_hidden_states, L0_attn_Wqkv_w, L0_attn_Wqkv_b,
           L0_attn_lambda_q1, L0_attn_lambda_k1, L0_attn_lambda_q2,
           L0_attn_lambda_k2, L0_attn_subln_w, L0_attn_out_w, L0_attn_out_b,
           L0_ffn_weight, L0_ffn_weight_scale, L0_ffn_bias):
    x = np.asarray(L0_attn_hidden_states)
    B, S, d = x.shape
    H = 16
    KV = 8
    cfg = Cfg(B=B, S=S, d=d, H=H, KV=KV, n_cores=8)
    nc = _get_nc(cfg)

    c = np.ascontiguousarray
    wqkvT = c(np.asarray(L0_attn_Wqkv_w).T)
    qkvb_t = c(np.asarray(L0_attn_Wqkv_b).reshape(cfg.OP // 128, 128).T)
    biasv_row = c(np.asarray(L0_attn_Wqkv_b)[H * 128 + KV * 128:].reshape(1, cfg.KVD))
    lam4 = c(np.stack([np.asarray(L0_attn_lambda_q1), np.asarray(L0_attn_lambda_k1),
                       np.asarray(L0_attn_lambda_q2), np.asarray(L0_attn_lambda_k2)],
                      axis=1))
    subln2 = c(np.asarray(L0_attn_subln_w).reshape(2, 128).T)
    outwT = c(np.asarray(L0_attn_out_w).T)
    outb_row = c(np.asarray(L0_attn_out_b).reshape(1, d))
    ffnT = c(np.asarray(L0_ffn_weight).T)
    ffnb_row = c(np.asarray(L0_ffn_bias).reshape(1, d))
    ws = c(np.asarray(L0_ffn_weight_scale).reshape(1, 1))

    in_maps = []
    for core in range(cfg.n_cores):
        in_maps.append({
            "xin": c(x[core * cfg.B_loc:(core + 1) * cfg.B_loc]),
            "wqkvT": wqkvT, "qkvb_t": qkvb_t, "biasv_row": biasv_row,
            "lam4": lam4, "subln2": subln2, "outwT": outwT, "outb_row": outb_row,
            "ffnT": ffnT, "ffnb_row": ffnb_row, "ws_in": ws,
        })
    res = run_bass_kernel_spmd(nc, in_maps, core_ids=list(range(cfg.n_cores)))
    return np.concatenate([r["xout"] for r in res.results], axis=0)


# version 38
# speedup vs baseline: 1.1252x; 1.0349x over previous
import math
from contextlib import ExitStack

import numpy as np
import ml_dtypes

import concourse.bass as bass
import concourse.mybir as mybir
from concourse.bass_utils import run_bass_kernel_spmd

F32 = mybir.dt.float32
BF16 = mybir.dt.bfloat16
F8 = mybir.dt.float8e4
AF = mybir.ActivationFunctionType
ALU = mybir.AluOpType
AX = mybir.AxisListType

LAMBDA_INIT = 0.8 - 0.6 * math.exp(-0.3)
ATT_SCALE = 128.0 ** -0.5
NEGBIG = -1.0e30
EPS = 1e-5
FP8_MAX = 448.0
SCALE_EPS = 1e-7


class Cfg:
    def __init__(self, B=16, S=2048, d=2048, H=16, KV=8, n_cores=8):
        self.B, self.S, self.d, self.H, self.KV, self.n_cores = B, S, d, H, KV, n_cores
        self.D = 128
        self.B_loc = B // n_cores
        self.nf = d // 128
        self.DH = H // 2
        self.G = KV // 2
        self.OP = (H + 2 * KV) * 128
        self.NQ = H
        self.NK = KV
        self.NE = H
        self.KVD = KV * 128
        self.NT = S // 128
        self.NT_ALL = self.B_loc * self.NT
        self.QG = 256
        assert self.OP % 256 == 0 and d % 256 == 0 and S % 512 == 0


class Buf:
    __slots__ = ("w", "r")

    def __init__(self, fence=None):
        self.w = list(fence) if fence else []
        self.r = []


class Ctx:
    ENGINES = ("t", "v", "a", "g", "y")

    def __init__(self):
        self.eng_ops = {e: [] for e in self.ENGINES}
        self.cnt = {}
        self.seen = {e: {} for e in self.ENGINES}
        self.semnames = []
        self.stream_last = {}
        self.fence = None

    def sem(self, name):
        if name not in self.cnt:
            self.cnt[name] = 0
            self.semnames.append(name)
        return name

    def buf(self):
        return Buf(self.fence)

    def set_fence(self):
        self.fence = [(s, v) for s, v in self.cnt.items() if v > 0]

    def _waits(self, eng, toks):
        best = {}
        for (s, v) in toks:
            if v > best.get(s, 0):
                best[s] = v
        out = []
        for s, v in best.items():
            if v > self.seen[eng].get(s, 0):
                out.append((s, v))
                self.seen[eng][s] = v
        return out

    def op(self, eng, fns, reads=(), writes=(), stream=None):
        if not isinstance(fns, (list, tuple)):
            fns = [fns]
        toks = []
        for b in reads:
            toks += b.w
        for b in writes:
            toks += b.w + b.r
        waits = self._waits(eng, toks)
        if stream is not None:
            semn = self.sem(stream)
            prev = self.stream_last.get(semn)
            if prev is not None:
                for wv in self._waits(eng, [prev]):
                    waits.append(wv)
            self.cnt[semn] += 16
            tok = (semn, self.cnt[semn])
            self.stream_last[semn] = tok
            amount = 16
        else:
            semn = self.sem("e" + eng)
            self.cnt[semn] += 1
            tok = (semn, self.cnt[semn])
            amount = 1
        self.eng_ops[eng].append((fns, waits, semn, amount))
        for b in reads:
            b.r.append(tok)
        for b in writes:
            b.w = [tok]
            b.r = []
        return tok

    def raw_wait(self, eng, toks):
        waits = self._waits(eng, toks)
        if waits:
            self.eng_ops[eng].append(([None], waits, None, 0))

    def emit(self, eng_obj, eng, handles):
        for fns, waits, semn, amount in self.eng_ops[eng]:
            fns_real = [f for f in fns if f is not None]
            if not fns_real:
                for (s, v) in waits:
                    eng_obj.wait_ge(handles[s], v)
                continue
            for (s, v) in waits[1:]:
                eng_obj.wait_ge(handles[s], v)
            for i, fn in enumerate(fns_real):
                inst = fn(eng_obj)
                if i == 0 and waits:
                    inst._wait_ge(handles[waits[0][0]], waits[0][1])
                if semn is not None and i == len(fns_real) - 1:
                    inst.then_inc(handles[semn], amount)


def _ln_smalls(ctx, d, sum_ap, ssq_ap, var_ap, inv_ap, negminv_ap, eps_ap,
               b_tmp, b_eps, b_keep=None, mean_ap=None):
    ALUl, AFl = ALU, AF
    bk = b_keep if b_keep is not None else b_tmp
    m_ap = mean_ap if mean_ap is not None else sum_ap
    ctx.op("v", (lambda o=m_ap, i=sum_ap, c=1.0 / d:
                 (lambda e: e.tensor_scalar_mul(o, i, c)))(),
           reads=[b_tmp], writes=[bk])
    ctx.op("v", (lambda o=ssq_ap, c=1.0 / d:
                 (lambda e: e.tensor_scalar_mul(o, o, c)))(),
           reads=[b_tmp], writes=[b_tmp])
    ctx.op("v", (lambda o=var_ap, m=m_ap:
                 (lambda e: e.scalar_tensor_tensor(o, m, -1.0, m,
                                                   ALUl.mult, ALUl.mult)))(),
           reads=[bk], writes=[b_tmp])
    ctx.op("v", (lambda o=var_ap, i=ssq_ap:
                 (lambda e: e.tensor_add(o, o, i)))(),
           reads=[b_tmp], writes=[b_tmp])
    ctx.op("a", (lambda o=inv_ap, i=var_ap, b=eps_ap:
                 (lambda e: e.activation(out=o, in_=i, func=AFl.Ln, bias=b)))(),
           reads=[b_tmp, b_eps], writes=[bk])
    ctx.op("a", (lambda o=inv_ap:
                 (lambda e: e.activation(out=o, in_=o, func=AFl.Exp, scale=-0.5)))(),
           reads=[bk], writes=[bk])
    if negminv_ap is not None:
        ctx.op("v", (lambda o=negminv_ap, m=m_ap, i=inv_ap:
                     (lambda e: e.scalar_tensor_tensor(o, m, -1.0, i,
                                                       ALUl.mult, ALUl.mult)))(),
               reads=[bk], writes=[b_tmp])


def build(cfg: Cfg):
    nc = bass.Bass("TRN2", target_bir_lowering=False)
    S, d, nf = cfg.S, cfg.d, cfg.nf
    NT, QG = cfg.NT, cfg.QG

    xin = nc.dram_tensor("xin", [cfg.B_loc, S, d], F32, kind="ExternalInput").ap()
    wqkvT = nc.dram_tensor("wqkvT", [d, cfg.OP], F32, kind="ExternalInput").ap()
    qkvb_t = nc.dram_tensor("qkvb_t", [128, cfg.OP // 128], F32, kind="ExternalInput").ap()
    biasv_row = nc.dram_tensor("biasv_row", [1, cfg.KVD], F32, kind="ExternalInput").ap()
    lam4 = nc.dram_tensor("lam4", [128, 4], F32, kind="ExternalInput").ap()
    subln2 = nc.dram_tensor("subln2", [128, 2], F32, kind="ExternalInput").ap()
    outwT = nc.dram_tensor("outwT", [cfg.NE * 128, d], F32, kind="ExternalInput").ap()
    outb_row = nc.dram_tensor("outb_row", [1, d], F32, kind="ExternalInput").ap()
    ffnT = nc.dram_tensor("ffnT", [d, d], F32, kind="ExternalInput").ap()
    ffnb_row = nc.dram_tensor("ffnb_row", [1, d], F32, kind="ExternalInput").ap()
    ws_in = nc.dram_tensor("ws_in", [1, 1], F32, kind="ExternalInput").ap()
    xout = nc.dram_tensor("xout", [cfg.B_loc, S, d], F32, kind="ExternalOutput").ap()

    qspill = nc.dram_tensor("qspill", [cfg.B_loc, cfg.NQ, 128, S], BF16).ap()
    kspill = nc.dram_tensor("kspill", [cfg.B_loc, cfg.NK, 128, S], BF16).ap()
    vspill = nc.dram_tensor("vspill", [cfg.B_loc, cfg.G, cfg.NT, 128, 256], BF16).ap()
    xmid_d = nc.dram_tensor("xmid_d", [cfg.NT_ALL, 128, d], F32).ap()
    cc_in = nc.dram_tensor("cc_in", [1, 1], F32)
    cc_out = nc.dram_tensor("cc_out", [1, 1], F32)

    id_bf_h = nc.inline_tensor(np.eye(128, dtype=ml_dtypes.bfloat16), "id_bf")
    id_f32_h = nc.inline_tensor(np.eye(128, dtype=np.float32), "id_f32")
    ut2 = np.zeros((128, 128), dtype=ml_dtypes.bfloat16)
    for dd in range(128):
        ut2[dd, : dd + 1] = 1.0
    ut2_h = nc.inline_tensor(ut2, "ut2")
    R_OFF = [0, 128, 384, 768]
    r4 = np.zeros((128, 1280), dtype=ml_dtypes.bfloat16)
    for a in range(4):
        for j in range(128 * a + 1, 128 * (a + 1)):
            r4[j - 128 * a - 1, R_OFF[a] + j] = NEGBIG
    r4_h = nc.inline_tensor(r4, "r4")
    ones_h = nc.inline_tensor(np.ones((128, 128), dtype=np.float32), "onesf")
    onesbf_h = nc.inline_tensor(np.ones((128, 2), dtype=ml_dtypes.bfloat16), "onesbf")

    ctx = Ctx()
    es = ExitStack()
    with es:
        sb = lambda name, shape, dt: es.enter_context(nc.sbuf_tensor(name, shape, dt))
        id_bf = sb("c_idbf", [128, 128], BF16)
        id_f32 = sb("c_idf32", [128, 128], F32)
        utb = sb("c_ut2", [128, 128], BF16)
        r4b = sb("c_r4", [128, 1280], BF16)
        onesb = sb("c_ones", [128, 128], F32)
        onesbf = sb("c_onesbf", [128, 2], BF16)
        qkvbb = sb("c_qkvb", [128, cfg.OP // 128], F32)
        lamb = sb("c_lam4", [128, 4], F32)
        sub2 = sb("c_sub2", [128, 2], F32)
        wCb = sb("c_wC", [128, 2], F32)
        epsb = sb("c_eps", [128, 1], F32)
        neglam_b = sb("c_neglam", [128, 1], F32)
        smallb = sb("c_smallb", [1, 8], F32)
        g2 = sb("c_g2", [128, 2], F32)
        wssb = sb("c_ws", [1, 1], F32)
        m2s = sb("s_mean2", [128, cfg.NT_ALL], F32)
        i2s = sb("s_inv2", [128, cfg.NT_ALL], F32)
        absm = sb("s_absm", [128, cfg.NT_ALL], F32)
        stats = [[sb(f"st{i}_{p}", [128, 1], F32) for i in range(6)]
                 for p in range(2)]

        B_ = lambda: ctx.buf()
        b_consts = B_()
        b_eps = B_()
        b_stats = B_()
        b_small = B_()
        b_stp = [B_(), B_()]
        b_qspill = B_()
        b_xmid = [B_() for _ in range(cfg.NT_ALL)]
        b_g2 = B_()
        b_out = B_()

        def dma(eng_name, stream, outap, inap, reads=(), writes=()):
            return ctx.op(eng_name,
                          (lambda o=outap, i=inap:
                           (lambda e: e.dma_start(out=o, in_=i)))(),
                          reads=reads, writes=writes, stream=stream)

        dma("y", "dc0", id_bf[:], id_bf_h.ap(), writes=[b_consts])
        dma("y", "dc0", id_f32[:], id_f32_h.ap(), writes=[b_consts])
        dma("y", "dc0", utb[:], ut2_h.ap(), writes=[b_consts])
        dma("y", "dc0", r4b[:], r4_h.ap(), writes=[b_consts])
        dma("y", "dc0", onesb[:], ones_h.ap(), writes=[b_consts])
        dma("y", "dc0", onesbf[:], onesbf_h.ap(), writes=[b_consts])
        dma("y", "dc0", qkvbb[:], qkvb_t, writes=[b_consts])
        dma("y", "dc0", lamb[:], lam4, writes=[b_consts])
        dma("y", "dc0", sub2[:], subln2, writes=[b_consts])
        dma("y", "dc0", wssb[:], ws_in, writes=[b_consts])

        ctx.op("v", lambda e: e.memset(epsb[:], EPS), writes=[b_eps])
        ctx.op("v", lambda e: e.tensor_scalar_mul(wCb[:], sub2[:], 1.0 - LAMBDA_INIT),
               reads=[b_consts], writes=[b_small])

        def mm(ps_ap, lhsT_ap, rhs_ap, start, stop):
            return (lambda p=ps_ap, l=lhsT_ap, r=rhs_ap, s1=start, s2=stop:
                    (lambda e: e.matmul(p, l, r, start=s1, stop=s2)))()

        def tr(ps_ap, in_ap, ident):
            return (lambda p=ps_ap, i=in_ap, d2=ident:
                    (lambda e: e.transpose(p, i, d2)))()

        def act(out_ap, in_ap, func, bias=0.0, scale=1.0, accum=None):
            return (lambda o=out_ap, i=in_ap, f=func, b=bias, s=scale, ac=accum:
                    (lambda e: e.activation(out=o, in_=i, func=f, bias=b, scale=s,
                                            accum_out=ac)))()

        def stt(out_ap, in0, scalar, in1, op0, op1, accum=None):
            return (lambda o=out_ap, a0=in0, sc=scalar, a1=in1, p0=op0, p1=op1, ac=accum:
                    (lambda e: e.scalar_tensor_tensor(o, a0, sc, a1, p0, p1,
                                                      accum_out=ac)))()

        def ts(out_ap, in0, s1, s2, op0, op1=ALU.bypass, accum=None):
            return (lambda o=out_ap, a0=in0, x1=s1, x2=s2, p0=op0, p1=op1, ac=accum:
                    (lambda e: e.tensor_scalar(o, a0, x1, x2, p0, p1, accum_out=ac)))()

        def red(out_ap, in_ap, op, absval=False):
            return (lambda o=out_ap, i=in_ap, p=op, av=absval:
                    (lambda e: e.tensor_reduce(o, i, AX.X, p,
                                               apply_absolute_value=av)))()

        def cp(out_ap, in_ap):
            return (lambda o=out_ap, i=in_ap: (lambda e: e.tensor_copy(o, i)))()

        with nc.psum_tensor("ps_lam", [128, 8], F32) as ps_lam:
            b_pl = B_()
            ctx.op("t", [mm(ps_lam[0:1, 0:1], lamb[:, 0:1], lamb[:, 1:2], True, True),
                         mm(ps_lam[0:1, 1:2], lamb[:, 2:3], lamb[:, 3:4], True, True)],
                   reads=[b_consts], writes=[b_pl])
            ctx.op("a", act(smallb[0:1, 0:2], ps_lam[0:1, 0:2], AF.Exp),
                   reads=[b_pl], writes=[b_small])
            ctx.op("v", (lambda e: e.tensor_sub(smallb[0:1, 2:3], smallb[0:1, 0:1],
                                                smallb[0:1, 1:2])),
                   reads=[b_small], writes=[b_small])
            ctx.op("v", ts(smallb[0:1, 3:4], smallb[0:1, 2:3], LAMBDA_INIT, -1.0,
                           ALU.add, ALU.mult),
                   reads=[b_small], writes=[b_small])
            ctx.op("t", mm(ps_lam[:, 2:3], onesb[0:1, :], smallb[0:1, 3:4], True, True),
                   reads=[b_small], writes=[b_pl])
            ctx.op("v", cp(neglam_b[:], ps_lam[:, 2:3]), reads=[b_pl], writes=[b_small])

        for seq in range(cfg.B_loc):
            ctx.set_fence()
            actA_g = nc.sbuf_tensor(f"actA_{seq}", [128, max(nf, cfg.NE) * S], BF16)
            actA = actA_g.__enter__()
            with ExitStack() as seq_es:
                ssb = lambda name, shape, dt: seq_es.enter_context(
                    nc.sbuf_tensor(name + f"_s{seq}", shape, dt))
                b_actA = B_()
                b_kspill, b_vspill = B_(), B_()
                actA3 = actA[:, 0:nf * S].rearrange("p (n c) -> p n c", c=S)

                ctx.set_fence()
                with ExitStack() as ab_es:
                    asb = lambda name, shape, dt: ab_es.enter_context(
                        nc.sbuf_tensor(name + f"_ab{seq}", shape, dt))
                    bufW = [asb("bufW0", [128, nf * 256], BF16),
                            asb("bufW1", [128, nf * 256], BF16)]
                    qst = [asb("qst0", [128, 512], BF16),
                           asb("qst1", [128, 512], BF16)]
                    vst = [asb("vst0", [128, 512], BF16),
                           asb("vst1", [128, 512], BF16)]
                    vbw = asb("vbw", [128, nf * 512], BF16)
                    b_vbw = B_()
                    tmt = [asb("tmt0", [128, d], BF16), asb("tmt1", [128, d], BF16)]
                    xt = [asb("xta", [128, d], F32), asb("xtb", [128, d], F32)]
                    biasvf = asb("biasvf", [128, cfg.KVD], BF16)
                    b_biasvf = B_()
                    b_xt = [B_(), B_()]
                    b_bufW = [B_(), B_()]
                    b_qst = [B_(), B_()]
                    b_vst = [B_(), B_()]
                    b_tmt = [B_(), B_()]

                    with nc.psum_tensor(f"psA_{seq}", [128, 1024], BF16) as psA:
                        b_psA = B_()
                        for t in range(NT):
                            par = t % 2
                            st = stats[par]
                            bsp = b_stp[par]
                            dma("y", f"dx{par}", xt[par][:],
                                xin[seq, t * 128:(t + 1) * 128, :], writes=[b_xt[par]])
                            ctx.op("a", act(tmt[par][:], xt[par][:], AF.Identity,
                                            accum=st[0][:]),
                                   reads=[b_xt[par]], writes=[b_tmt[par], bsp])
                            ctx.op("a", act(tmt[par][:], xt[par][:], AF.Square,
                                            accum=st[1][:]),
                                   reads=[b_xt[par]], writes=[b_tmt[par], bsp])
                            _ln_smalls(ctx, d, st[0][:], st[1][:], st[2][:], st[3][:],
                                       st[4][:], epsb[:], bsp, b_eps)
                            ctx.op("a", act(tmt[par][:], xt[par][:], AF.Identity,
                                            bias=st[4][:], scale=st[3][:]),
                                   reads=[b_xt[par], bsp], writes=[b_tmt[par]])
                            for k0 in range(0, nf, 8):
                                kn = min(8, nf - k0)
                                ctx.op("t",
                                       [tr(psA[:, (k - k0) * 128:(k - k0 + 1) * 128],
                                           tmt[par][:, k * 128:(k + 1) * 128], id_bf[:])
                                        for k in range(k0, k0 + kn)],
                                       reads=[b_tmt[par], b_consts], writes=[b_psA])
                                src = psA[:, 0:kn * 128].rearrange(
                                    "p (n c) -> p n c", c=128)
                                dst = actA3[:, k0:k0 + kn, t * 128:(t + 1) * 128]
                                ctx.op("v", cp(dst, src), reads=[b_psA],
                                       writes=[b_actA])

                    ctx.set_fence()
                    with ExitStack() as ps_es:
                        psB = [ps_es.enter_context(
                            nc.psum_tensor(f"psB{i}_{seq}", [128, 512], F32))
                            for i in range(8)]
                        b_psB = [B_() for _ in range(8)]
                        with nc.sbuf_tensor(f"bvrow_{seq}", [1, cfg.KVD], F32) as bvr:
                            b_bvr = B_()
                            dma("y", "dc0", bvr[:], biasv_row, writes=[b_bvr])
                            for j in range(0, cfg.KVD, 512):
                                w = min(512, cfg.KVD - j)
                                ctx.op("t", mm(psB[7][:, 0:w], onesb[0:1, :],
                                               bvr[0:1, j:j + w], True, True),
                                       reads=[b_bvr], writes=[b_psB[7]])
                                ctx.op("v", cp(biasvf[:, j:j + w], psB[7][:, 0:w]),
                                       reads=[b_psB[7]], writes=[b_biasvf])
                        ctx.set_fence()
                        nslab = cfg.OP // 256
                        nqk_slab = (cfg.NQ + cfg.NK) // 2
                        for s_i in range(nslab):
                            par = s_i % 2
                            if s_i < nqk_slab:
                                src = wqkvT[:, s_i * 256:(s_i + 1) * 256] \
                                    .rearrange("(c p) o -> p c o", p=128)
                                dst_w = bufW[par][:].rearrange(
                                    "p (c o) -> p c o", o=256)
                                dma("g", f"dw{par}", dst_w, src,
                                    writes=[b_bufW[par]])
                                for j in range(2):
                                    oc = 2 * s_i + j
                                    base = 4 * (oc % 2)
                                    grp = [psB[base + q] for q in range(4)]
                                    bgrp = [b_psB[base + q] for q in range(4)]
                                    fns = []
                                    for ic in range(nf):
                                        for t4 in range(S // 512):
                                            fns.append(mm(
                                                grp[t4][:],
                                                bufW[par][:, ic * 256 + j * 128:
                                                          ic * 256 + j * 128 + 128],
                                                actA[:, ic * S + t4 * 512:
                                                     ic * S + (t4 + 1) * 512],
                                                ic == 0, ic == nf - 1))
                                    ctx.op("t", fns, reads=[b_bufW[par], b_actA],
                                           writes=bgrp)
                                    for t4 in range(S // 512):
                                        if oc < cfg.NQ:
                                            qp = t4 % 2
                                            ctx.op("a", act(qst[qp][:], grp[t4][:],
                                                            AF.Identity,
                                                            bias=qkvbb[:, oc:oc + 1]),
                                                   reads=[bgrp[t4], b_consts],
                                                   writes=[b_qst[qp]])
                                            dma("y", f"dq{qp}",
                                                qspill[seq, oc, :,
                                                       t4 * 512:(t4 + 1) * 512],
                                                qst[qp][:], reads=[b_qst[qp]],
                                                writes=[b_qspill])
                                        else:
                                            kc = oc - cfg.NQ
                                            qp = t4 % 2
                                            ctx.op("a", act(qst[qp][:], grp[t4][:],
                                                            AF.Identity,
                                                            bias=qkvbb[:, oc:oc + 1]),
                                                   reads=[bgrp[t4], b_consts],
                                                   writes=[b_qst[qp]])
                                            dma("y", f"dq{qp}",
                                                kspill[seq, kc, :,
                                                       t4 * 512:(t4 + 1) * 512],
                                                qst[qp][:], reads=[b_qst[qp]],
                                                writes=[b_kspill])
                            else:
                                g4v = s_i - nqk_slab
                                nvs = nslab - nqk_slab
                                if g4v % 2 == 0 and g4v + 1 < nvs:
                                    srcv = wqkvT[:, s_i * 256:(s_i + 2) * 256] \
                                        .rearrange("(c p) o -> p c o", p=128)
                                    dma("g", f"dw{par}",
                                        vbw[:].rearrange("p (c o) -> p c o", o=512),
                                        srcv, writes=[b_vbw])
                                    for t in range(NT):
                                        pv = psB[t % 2]
                                        bpv = b_psB[t % 2]
                                        vp = t % 2
                                        fns = [mm(pv[:, 0:512],
                                                  actA[:, ic * S + t * 128:
                                                       ic * S + (t + 1) * 128],
                                                  vbw[:, ic * 512:(ic + 1) * 512],
                                                  ic == 0, ic == nf - 1)
                                               for ic in range(nf)]
                                        ctx.op("t", fns, reads=[b_vbw, b_actA],
                                               writes=[bpv])
                                        ctx.op("v", stt(
                                            vst[vp][:], pv[:, 0:512], 1.0,
                                            biasvf[:, g4v * 256:g4v * 256 + 512],
                                            ALU.mult, ALU.add),
                                            reads=[bpv, b_biasvf],
                                            writes=[b_vst[vp]])
                                        vdst = vspill[seq, g4v:g4v + 2, t, :, :] \
                                            .rearrange("g p v -> p g v")
                                        dma("y", f"dq{vp}", vdst,
                                            vst[vp][:].rearrange(
                                                "p (g v) -> p g v", v=256),
                                            reads=[b_vst[vp]],
                                            writes=[b_vspill])
                                elif g4v % 2 == 0:
                                    srcv = wqkvT[:, s_i * 256:(s_i + 1) * 256] \
                                        .rearrange("(c p) o -> p c o", p=128)
                                    dst_w = bufW[par][:].rearrange(
                                        "p (c o) -> p c o", o=256)
                                    dma("g", f"dw{par}", dst_w, srcv,
                                        writes=[b_bufW[par]])
                                    for t in range(NT):
                                        pv = psB[t % 2]
                                        bpv = b_psB[t % 2]
                                        vp = t % 2
                                        fns = [mm(pv[:, 0:256],
                                                  actA[:, ic * S + t * 128:
                                                       ic * S + (t + 1) * 128],
                                                  bufW[par][:, ic * 256:
                                                            (ic + 1) * 256],
                                                  ic == 0, ic == nf - 1)
                                               for ic in range(nf)]
                                        ctx.op("t", fns,
                                               reads=[b_bufW[par], b_actA],
                                               writes=[bpv])
                                        ctx.op("v", stt(
                                            vst[vp][:, 0:256], pv[:, 0:256], 1.0,
                                            biasvf[:, g4v * 256:g4v * 256 + 256],
                                            ALU.mult, ALU.add),
                                            reads=[bpv, b_biasvf],
                                            writes=[b_vst[vp]])
                                        dma("y", f"dq{vp}",
                                            vspill[seq, g4v, t, :, :],
                                            vst[vp][:, 0:256],
                                            reads=[b_vst[vp]],
                                            writes=[b_vspill])

                ctx.set_fence()
                with ExitStack() as c_es:
                    csb = lambda name, shape, dt: c_es.enter_context(
                        nc.sbuf_tensor(name + f"_c{seq}", shape, dt))
                    ckp = [csb("ck0", [128, 2 * S], BF16),
                           csb("ck1", [128, 2 * S], BF16)]
                    cvp = [csb("cv0", [128, NT * 256], BF16),
                           csb("cv1", [128, NT * 256], BF16)]
                    cq = [csb(f"cq{i}", [128, 2 * S], BF16) for i in range(4)]
                    e1 = [[csb(f"e1_{L}{q}", [128, S], BF16) for q in range(2)]
                          for L in range(2)]
                    e2 = [[csb(f"e2_{L}{q}", [128, S], BF16) for q in range(2)]
                          for L in range(2)]
                    pbuf = [[csb(f"pb_{L}{q}", [128, S], BF16) for q in range(2)]
                            for L in range(2)]
                    bufPT = [csb("bufPT0", [128, NT * QG], BF16),
                             csb("bufPT1", [128, NT * QG], BF16)]
                    osq = [csb("osq0", [128, 2 * QG], BF16),
                           csb("osq1", [128, 2 * QG], BF16)]
                    o_sb = [csb("osb0", [128, 2 * QG], BF16),
                            csb("osb1", [128, 2 * QG], BF16)]
                    S_sb = [csb("S_sb0", [128, QG], F32),
                            csb("S_sb1", [128, QG], F32)]
                    zb = [[csb(f"zb{L}{q}", [128, 8], F32) for q in range(2)]
                          for L in range(2)]
                    zs = [[csb(f"zs{L}{q}", [128, 2], F32) for q in range(2)]
                          for L in range(2)]
                    a2s = [[csb(f"a2{L}{q}", [128, 1], F32) for q in range(2)]
                           for L in range(2)]
                    b_ckp = [B_(), B_()]
                    b_cvp = [B_(), B_()]
                    b_cq = [B_() for _ in range(4)]
                    b_e1 = [[B_(), B_()], [B_(), B_()]]
                    b_e2 = [[B_(), B_()], [B_(), B_()]]
                    b_pb = [[B_(), B_()], [B_(), B_()]]
                    b_PT = [B_(), B_()]
                    b_osq = [B_(), B_()]
                    b_osb = [B_(), B_()]
                    b_Ssb = [B_(), B_()]
                    b_z = [[B_(), B_()], [B_(), B_()]]
                    b_zs = [[B_(), B_()], [B_(), B_()]]
                    bufPT3 = [bufPT[i][:].rearrange("p (n c) -> p n c", c=QG)
                              for i in range(2)]

                    with ExitStack() as ps_es:
                        psS = [ps_es.enter_context(
                            nc.psum_tensor(f"psS{i}_{seq}", [128, 1024], F32))
                            for i in range(2)]
                        psO = [ps_es.enter_context(
                            nc.psum_tensor(f"psO{i}_{seq}", [128, 2 * QG], F32))
                            for i in range(2)]
                        psPT = [ps_es.enter_context(
                            nc.psum_tensor(f"psPT{i}_{seq}", [128, 1024], BF16))
                            for i in range(2)]
                        b_psS = [[B_(), B_()], [B_(), B_()]]
                        b_psO = [B_(), B_()]
                        b_psPT = [B_(), B_()]

                        def kv_load(hp):
                            pp = hp % 2
                            ksrc = kspill[seq, 2 * hp:2 * hp + 2, :, :] \
                                .rearrange("c p t -> p c t")
                            dma("y", "dql0",
                                ckp[pp][:].rearrange("p (c t) -> p c t", t=S),
                                ksrc, reads=[b_kspill], writes=[b_ckp[pp]])
                            vsrc = vspill[seq, hp, :, :, :] \
                                .rearrange("t p v -> p t v")
                            dma("y", "dql1",
                                cvp[pp][:].rearrange("p (t v) -> p t v", v=256),
                                vsrc, reads=[b_vspill], writes=[b_cvp[pp]])

                        def q_load(h):
                            qi = (h % 2) * 2 + (h // 2) % 2
                            src = qspill[seq, 2 * h:2 * h + 2, :, :] \
                                .rearrange("c p t -> p c t")
                            dma("y", "dql2",
                                cq[qi][:].rearrange("p (c t) -> p c t", t=S),
                                src, reads=[b_qspill], writes=[b_cq[qi]])

                        def scores_pair(hp, qt):
                            pp = hp % 2
                            kmax = (qt + 1) * 128
                            nfull = (qt * 128) // 512
                            nkc = nfull + 1
                            aa = qt % 4
                            wrem = kmax - nfull * 512
                            qp = qt % 2
                            for kc in range(nkc):
                                w0 = 512 if kc < nfull else wrem
                                for L in range(2):
                                    qi = L * 2 + hp % 2
                                    for comp in range(2):
                                        eX = e1[L][qp] if comp == 0 else e2[L][qp]
                                        bX = (b_e1[L][qp] if comp == 0
                                              else b_e2[L][qp])
                                        lhq = cq[qi][:, comp * S + qt * 128:
                                                     comp * S + (qt + 1) * 128]
                                        kbase = comp * S
                                        fns = [mm(psS[L][:, comp * 512:
                                                         comp * 512 + w0],
                                                  lhq,
                                                  ckp[pp][:, kbase + kc * 512:
                                                      kbase + kc * 512 + w0],
                                                  True, kc != nfull)]
                                        rd = [b_cq[qi], b_ckp[pp]]
                                        if kc == nfull:
                                            fns.append(mm(
                                                psS[L][:, comp * 512:
                                                       comp * 512 + w0],
                                                utb[:],
                                                r4b[:, R_OFF[aa]:R_OFF[aa] + w0],
                                                False, True))
                                            rd.append(b_consts)
                                        ctx.op("t", fns, reads=rd,
                                               writes=[b_psS[L][comp]])
                                        ctx.op("a", act(
                                            eX[:, kc * 512:kc * 512 + w0],
                                            psS[L][:, comp * 512:comp * 512 + w0],
                                            AF.Exp, scale=ATT_SCALE,
                                            accum=zb[L][qp][:, comp * 4 + kc:
                                                            comp * 4 + kc + 1]),
                                            reads=[b_psS[L][comp]],
                                            writes=[bX, b_z[L][qp]])

                        def comb(L, qt):
                            qp = qt % 2
                            kmax = (qt + 1) * 128
                            nkc = (qt * 128) // 512 + 1
                            zb3 = zb[L][qp][:].rearrange("p (c q) -> p c q",
                                                         q=4)[:, :, 0:nkc]
                            ctx.op("v", red(zs[L][qp][:], zb3, ALU.add),
                                   reads=[b_z[L][qp]], writes=[b_zs[L][qp]])
                            ctx.op("v", (lambda o=zs[L][qp][:]:
                                         (lambda e: e.reciprocal(o, o)))(),
                                   reads=[b_zs[L][qp]], writes=[b_zs[L][qp]])
                            ctx.op("v", (lambda o=a2s[L][qp][:],
                                         i0=zs[L][qp][:, 1:2],
                                         i1=neglam_b[:]:
                                         (lambda e: e.tensor_mul(o, i0, i1)))(),
                                   reads=[b_zs[L][qp], b_small],
                                   writes=[b_zs[L][qp]])
                            ctx.op("v", ts(pbuf[L][qp][:, 0:kmax],
                                           e1[L][qp][:, 0:kmax],
                                           zs[L][qp][:, 0:1], None, ALU.mult),
                                   reads=[b_e1[L][qp], b_zs[L][qp]],
                                   writes=[b_pb[L][qp]])
                            ctx.op("v", stt(pbuf[L][qp][:, 0:kmax],
                                            e2[L][qp][:, 0:kmax],
                                            a2s[L][qp][:], pbuf[L][qp][:, 0:kmax],
                                            ALU.mult, ALU.add),
                                   reads=[b_e2[L][qp], b_zs[L][qp], b_pb[L][qp]],
                                   writes=[b_pb[L][qp]])

                        def tr_qt(L, qt):
                            qp = qt % 2
                            if qt % 2 == 0:
                                kbz = qt + 1
                                ctx.op("v", (lambda o=bufPT3[L][:, kbz:kbz + 1,
                                                              0:128]:
                                             (lambda e: e.memset(o, 0.0)))(),
                                       writes=[b_PT[L]])
                            qoff = (qt % 2) * 128
                            nkb = qt + 1
                            for k0 in range(0, nkb, 8):
                                kn = min(8, nkb - k0)
                                ctx.op("t",
                                       [tr(psPT[L][:, (kb - k0) * 128:
                                               (kb - k0 + 1) * 128],
                                           pbuf[L][qp][:, kb * 128:(kb + 1) * 128],
                                           id_bf[:])
                                        for kb in range(k0, k0 + kn)],
                                       reads=[b_pb[L][qp], b_consts],
                                       writes=[b_psPT[L]])
                                src2 = psPT[L][:, 0:kn * 128].rearrange(
                                    "p (n c) -> p n c", c=128)
                                dst2 = bufPT3[L][:, k0:k0 + kn, qoff:qoff + 128]
                                ctx.op("v", cp(dst2, src2),
                                       reads=[b_psPT[L]], writes=[b_PT[L]])

                        def o_mms(hp, L, qg):
                            pp = hp % 2
                            nkb_g = 2 * qg + 2
                            fns = []
                            for ec in range(2):
                                for kb in range(nkb_g):
                                    fns.append(mm(
                                        psO[L][:, ec * QG:(ec + 1) * QG],
                                        cvp[pp][:, kb * 256 + ec * 128:
                                            kb * 256 + ec * 128 + 128],
                                        bufPT[L][:, kb * QG:(kb + 1) * QG],
                                        kb == 0, kb == nkb_g - 1))
                            ctx.op("t", fns, reads=[b_cvp[pp], b_PT[L]],
                                   writes=[b_psO[L]])
                            ctx.op("a", act(osq[L][:], psO[L][:], AF.Square),
                                   reads=[b_psO[L]], writes=[b_osq[L]])
                            ctx.op("a", act(o_sb[L][:], psO[L][:], AF.Identity),
                                   reads=[b_psO[L]], writes=[b_osb[L]])

                        def o_fin(hp, L, qg):
                            h = 2 * hp + L
                            ctx.op("t", [mm(psO[L][0:1, 0:QG], onesbf[:, 0:1],
                                            osq[L][:, 0:QG], True, False),
                                         mm(psO[L][0:1, 0:QG], onesbf[:, 0:1],
                                            osq[L][:, QG:2 * QG], False, True)],
                                   reads=[b_osq[L], b_consts], writes=[b_psO[L]])
                            ctx.op("a", act(S_sb[L][0:1, 0:QG], psO[L][0:1, 0:QG],
                                            AF.Ln, bias=epsb[0:1, :],
                                            scale=1.0 / 256.0),
                                   reads=[b_psO[L], b_eps], writes=[b_Ssb[L]])
                            ctx.op("a", act(S_sb[L][0:1, 0:QG], S_sb[L][0:1, 0:QG],
                                            AF.Exp, scale=-0.5),
                                   reads=[b_Ssb[L]], writes=[b_Ssb[L]])
                            ctx.op("t", mm(psO[L][:, 0:QG], onesb[0:1, :],
                                           S_sb[L][0:1, 0:QG], True, True),
                                   reads=[b_Ssb[L]], writes=[b_psO[L]])
                            ctx.op("v", cp(S_sb[L][:], psO[L][:, 0:QG]),
                                   reads=[b_psO[L]], writes=[b_Ssb[L]])
                            for ec in range(2):
                                dst3 = actA[:, (2 * h + ec) * S + qg * QG:
                                            (2 * h + ec) * S + (qg + 1) * QG]
                                ctx.op("v", stt(dst3,
                                                o_sb[L][:, ec * QG:(ec + 1) * QG],
                                                wCb[:, ec:ec + 1], S_sb[L][:],
                                                ALU.mult, ALU.mult),
                                       reads=[b_osb[L], b_small, b_Ssb[L]],
                                       writes=[b_actA])

                        kv_load(0)
                        q_load(0)
                        q_load(1)
                        for hp in range(cfg.DH // 2):
                            if hp + 1 < cfg.DH // 2:
                                kv_load(hp + 1)
                                q_load(2 * hp + 2)
                                q_load(2 * hp + 3)
                            scores_pair(hp, 0)
                            pend = None
                            for qt in range(NT):
                                comb(0, qt)
                                comb(1, qt)
                                if qt + 1 < NT:
                                    scores_pair(hp, qt + 1)
                                if pend is not None:
                                    o_fin(hp, 0, pend)
                                    o_fin(hp, 1, pend)
                                    pend = None
                                tr_qt(0, qt)
                                tr_qt(1, qt)
                                if qt % 2 == 1:
                                    o_mms(hp, 0, qt // 2)
                                    o_mms(hp, 1, qt // 2)
                                    pend = qt // 2
                            o_fin(hp, 0, pend)
                            o_fin(hp, 1, pend)

            ctx.set_fence()
            with ExitStack() as d_es:
                dsb = lambda name, shape, dt: d_es.enter_context(
                    nc.sbuf_tensor(name + f"_d{seq}", shape, dt))
                noc4 = (d + 511) // 512
                ow4 = min(512, d)
                owt = [dsb(f"owt{i}", [128, cfg.NE * ow4], BF16)
                       for i in range(noc4)]
                at_tm = [dsb("atta", [128, d], BF16), dsb("attb", [128, d], BF16)]
                xt = [dsb("xta", [128, d], F32), dsb("xtb", [128, d], F32)]
                b_xt = [B_(), B_()]
                bfull = dsb("bfull", [128, d], F32)
                xm1s = dsb("xm1", [128, d], F32)
                xm1 = [xm1s, xm1s]
                xms = [dsb("xms0", [128, d], F32), dsb("xms1", [128, d], F32)]
                sp4 = [dsb("sp4a", [128, 4], F32), dsb("sp4b", [128, 4], F32)]
                jnkD = xm1
                b_owt = [B_() for _ in range(noc4)]
                b_att = [B_(), B_()]
                b_bfull = B_()
                b_xm1s = B_()
                b_xm1 = [b_xm1s, b_xm1s]
                b_xms = [B_(), B_()]
                b_jnk = b_xm1
                for i4 in range(noc4):
                    w4s = min(512, d - i4 * 512)
                    srcs = outwT[:, i4 * 512:i4 * 512 + w4s] \
                        .rearrange("(c p) o -> p c o", p=128)
                    dma("g", f"dw{i4 % 2}",
                        owt[i4][:, 0:cfg.NE * w4s].rearrange(
                            "p (c o) -> p c o", o=w4s),
                        srcs, writes=[b_owt[i4]])
                ctx.set_fence()
                with ExitStack() as ps_es:
                    psD = [ps_es.enter_context(
                        nc.psum_tensor(f"psD{i}_{seq}", [128, 512], F32))
                        for i in range(8)]
                    b_psD = [B_() for _ in range(8)]
                    with nc.sbuf_tensor(f"obrow_{seq}", [1, d], F32) as obrow:
                        b_obr = B_()
                        dma("y", "dc0", obrow[:], outb_row, writes=[b_obr])
                        for j in range(0, d, 512):
                            wj = min(512, d - j)
                            ctx.op("t", mm(psD[0][:, 0:wj], onesb[0:1, :],
                                           obrow[0:1, j:j + wj], True, True),
                                   reads=[b_obr], writes=[b_psD[0]])
                            ctx.op("v", cp(bfull[:, j:j + wj], psD[0][:, 0:wj]),
                                   reads=[b_psD[0]], writes=[b_bfull])
                    ctx.set_fence()

                    noc4 = (d + 511) // 512

                    def d_mm(t):
                        bank0 = 4 * (t % 2)
                        for oc4 in range(noc4):
                            w4 = min(512, d - oc4 * 512)
                            pp = psD[bank0 + oc4]
                            bpp = b_psD[bank0 + oc4]
                            fns = [mm(pp[:, 0:w4],
                                      actA[:, ec * S + t * 128:ec * S + (t + 1) * 128],
                                      owt[oc4][:, ec * w4:ec * w4 + w4],
                                      ec == 0, ec == cfg.NE - 1)
                                   for ec in range(cfg.NE)]
                            ctx.op("t", fns, reads=[b_owt[oc4], b_actA],
                                   writes=[bpp])

                    def d_ln(t):
                        gt = seq * NT + t
                        par = t % 2
                        bank0 = 4 * (t % 2)
                        st = stats[par]
                        bsp = b_stp[par]
                        for oc4 in range(noc4):
                            w4 = min(512, d - oc4 * 512)
                            ctx.op("v", stt(at_tm[par][:, oc4 * 512:oc4 * 512 + w4],
                                            psD[bank0 + oc4][:, 0:w4], 1.0,
                                            bfull[:, oc4 * 512:oc4 * 512 + w4],
                                            ALU.mult, ALU.add,
                                            accum=sp4[par][:, oc4:oc4 + 1]),
                                   reads=[b_psD[bank0 + oc4], b_bfull],
                                   writes=[b_att[par], bsp])
                        dma("y", f"dx{par}", xt[par][:],
                            xin[seq, t * 128:(t + 1) * 128, :], writes=[b_xt[par]])
                        a_ap = at_tm[par][:]
                        ctx.op("v", red(st[0][:], sp4[par][:, 0:noc4], ALU.add),
                               reads=[bsp], writes=[bsp])
                        ctx.op("a", act(jnkD[par][:], a_ap, AF.Square,
                                        accum=st[1][:]),
                               reads=[b_att[par]], writes=[b_jnk[par], bsp])
                        _ln_smalls(ctx, d, st[0][:], st[1][:], st[2][:], st[3][:],
                                   st[4][:], epsb[:], bsp, b_eps)
                        ctx.op("v", stt(xm1[par][:], a_ap, st[3][:], xt[par][:],
                                        ALU.mult, ALU.add),
                               reads=[b_att[par], bsp, b_xt[par]],
                               writes=[b_xm1[par]])
                        ctx.op("a", act(xms[par][:], xm1[par][:], AF.Identity,
                                        bias=st[4][:], accum=st[0][:]),
                               reads=[b_xm1[par], bsp], writes=[b_xms[par], bsp])
                        ctx.op("a", act(jnkD[par][:], xms[par][:], AF.Square,
                                        accum=st[1][:]),
                               reads=[b_xms[par]], writes=[b_jnk[par], bsp])
                        ctx.op("v", red(st[5][:], xms[par][:], ALU.max),
                               reads=[b_xms[par]], writes=[bsp])
                        ctx.op("v", red(st[2][:], xms[par][:], ALU.min),
                               reads=[b_xms[par]], writes=[bsp])
                        dma("y", f"dm{par}", xmid_d[gt, :, :], xms[par][:],
                            reads=[b_xms[par]], writes=[b_xmid[gt]])
                        _ln_smalls(ctx, d, st[0][:], st[1][:], st[3][:],
                                   i2s[:, gt:gt + 1], None, epsb[:], bsp, b_eps,
                                   b_keep=b_stats, mean_ap=m2s[:, gt:gt + 1])
                        ctx.op("v", (lambda o=st[5][:], m=m2s[:, gt:gt + 1]:
                                     (lambda e: e.tensor_sub(o, o, m)))(),
                               reads=[bsp, b_stats], writes=[bsp])
                        ctx.op("v", (lambda o=st[2][:], m=m2s[:, gt:gt + 1]:
                                     (lambda e: e.tensor_sub(o, m, o)))(),
                               reads=[bsp, b_stats], writes=[bsp])
                        ctx.op("v", (lambda o=st[5][:], i=st[2][:]:
                                     (lambda e: e.tensor_max(o, o, i)))(),
                               reads=[bsp], writes=[bsp])
                        ctx.op("v", (lambda o=absm[:, gt:gt + 1], a0=st[5][:],
                                     i=i2s[:, gt:gt + 1]:
                                     (lambda e: e.tensor_mul(o, a0, i)))(),
                               reads=[bsp, b_stats], writes=[b_stats])

                    for i in range(NT + 1):
                        if i < NT:
                            d_mm(i)
                        if i >= 1:
                            d_ln(i - 1)
            actA_g.__exit__(None, None, None)

        actA_fence = list(b_actA.w) + list(b_actA.r)
        with ExitStack() as f_es:
            fsb = lambda name, shape, dt: f_es.enter_context(
                nc.sbuf_tensor(name, shape, dt))
            wf = fsb("wf", [128, nf * d], BF16)
            xtf = [fsb(f"xtf{i}", [128, d], F32) for i in range(4)]
            fbias = fsb("fbias", [128, d], F32)
            b_wf = Buf(actA_fence)
            b_xtf = [Buf(actA_fence) for _ in range(4)]
            b_fbias = Buf(actA_fence)
            ctx.set_fence()
            xq8 = [fsb(f"xq8{i}", [128, d], F8) for i in range(4)]
            xqb = [fsb(f"xqb{i}", [128, d], BF16) for i in range(4)]
            xqf = [fsb(f"xqf{i}", [128, nf * 128], BF16) for i in range(4)]
            fb = [fsb("fba", [128, d], F32), fsb("fbb", [128, d], F32)]
            ot1s = fsb("ot1", [128, d], F32)
            ot1 = [ot1s, ot1s]
            ost = [fsb("ost0", [128, d], F32), fsb("ost1", [128, d], F32)]
            sct = [fsb(f"sct{i}", [128, 1], F32) for i in range(4)]
            sp4f = [fsb("sp4fa", [128, 4], F32), fsb("sp4fb", [128, 4], F32)]
            sbt = [fsb(f"sbt{i}", [128, 1], F32) for i in range(4)]
            b_xq8 = [B_() for _ in range(4)]
            b_xqb = [B_() for _ in range(4)]
            b_xqf = [B_() for _ in range(4)]
            b_fb = [B_(), B_()]
            b_ot1s = B_()
            b_ot1 = [b_ot1s, b_ot1s]
            b_ost = [B_(), B_()]
            b_sc = [B_() for _ in range(4)]

            src = ffnT.rearrange("(c p) o -> p c o", p=128)
            dma("g", "dw1", wf[:].rearrange("p (c o) -> p c o", o=d), src,
                writes=[b_wf])

            b_cc = B_()
            with nc.psum_tensor("psX", [128, 128], F32) as psX:
                b_psX = B_()
                ctx.op("v", red(stats[0][0][:], absm[:, 0:cfg.NT_ALL], ALU.max),
                       reads=[b_stats], writes=[b_stp[0]])
                ctx.op("t", mm(psX[0:1, :], stats[0][0][:], id_f32[:], True, True),
                       reads=[b_stp[0], b_consts], writes=[b_psX])
                ctx.op("v", red(smallb[0:1, 0:1], psX[0:1, :], ALU.max),
                       reads=[b_psX], writes=[b_small])
                dma("g", "dcc0", cc_in.ap(), smallb[0:1, 0:1],
                    reads=[b_small], writes=[b_cc])
                ctx.op("g", (lambda e: e.collective_compute(
                    "AllReduce", ALU.max,
                    replica_groups=[list(range(cfg.n_cores))],
                    ins=[cc_in.ap().opt()], outs=[cc_out.ap().opt()])),
                       reads=[b_cc], writes=[b_cc])
                dma("g", "dcc0", smallb[0:1, 1:2], cc_out.ap(),
                    reads=[b_cc], writes=[b_small])
                ctx.op("v", ts(smallb[0:1, 2:3], smallb[0:1, 1:2], 1.0 / FP8_MAX,
                               SCALE_EPS, ALU.mult, ALU.max),
                       reads=[b_small], writes=[b_small])
                ctx.op("v", (lambda o=smallb[0:1, 2:3]:
                             (lambda e: e.reciprocal(o, o)))(),
                       reads=[b_small], writes=[b_small])
                ctx.op("v", ts(smallb[0:1, 2:3], smallb[0:1, 2:3], 0.5, None,
                               ALU.mult),
                       reads=[b_small], writes=[b_small])
                ctx.op("v", ts(smallb[0:1, 3:4], wssb[0:1, :], 2.0, None, ALU.mult),
                       reads=[b_consts], writes=[b_small])
                ctx.op("t", [mm(psX[:, 0:1], onesb[0:1, :], smallb[0:1, 2:3],
                                True, True),
                             mm(psX[:, 1:2], onesb[0:1, :], smallb[0:1, 3:4],
                                True, True)],
                       reads=[b_small], writes=[b_psX])
                ctx.op("v", cp(g2[:], psX[:, 0:2]), reads=[b_psX], writes=[b_g2])

            ctx.set_fence()
            with ExitStack() as ps_es:
                psF = [ps_es.enter_context(
                    nc.psum_tensor(f"psF{i}", [128, 512], F32)) for i in range(4)]
                psPT3 = [ps_es.enter_context(
                    nc.psum_tensor(f"psPT3{i}", [128, 1024], BF16))
                    for i in range(4)]
                b_psF = [B_() for _ in range(4)]
                b_psPT3 = [B_() for _ in range(4)]
                with nc.sbuf_tensor("fbrow", [1, d], F32) as fbrow:
                    b_fbrow = B_()
                    dma("y", "dc0", fbrow[:], ffnb_row, writes=[b_fbrow])
                    for j in range(0, d, 512):
                        wj = min(512, d - j)
                        ctx.op("t", mm(psF[0][:, 0:wj], onesb[0:1, :],
                                       fbrow[0:1, j:j + wj], True, True),
                               reads=[b_fbrow], writes=[b_psF[0]])
                        ctx.op("v", cp(fbias[:, j:j + wj], psF[0][:, 0:wj]),
                               reads=[b_psF[0]], writes=[b_fbias])
                ctx.set_fence()

                def f_prep_a(t):
                    p4 = t % 4
                    dma("y", f"dxf{p4}", xtf[p4][:], xmid_d[t, :, :],
                        reads=[b_xmid[t]], writes=[b_xtf[p4]])
                    ctx.op("v", (lambda o=sct[p4][:], a0=i2s[:, t:t + 1],
                                 i=g2[:, 0:1]:
                                 (lambda e: e.tensor_mul(o, a0, i)))(),
                           reads=[b_stats, b_g2], writes=[b_sc[p4]])
                    ctx.op("v", stt(sbt[p4][:], m2s[:, t:t + 1], -1.0, sct[p4][:],
                                    ALU.mult, ALU.mult),
                           reads=[b_stats, b_sc[p4]], writes=[b_sc[p4]])
                    ctx.op("a", act(xq8[p4][:], xtf[p4][:], AF.Identity,
                                    bias=sbt[p4][:], scale=sct[p4][:]),
                           reads=[b_xtf[p4], b_sc[p4]], writes=[b_xq8[p4]])
                    ctx.op("a", act(xqb[p4][:], xq8[p4][:], AF.Identity),
                           reads=[b_xq8[p4]], writes=[b_xqb[p4]])

                def f_prep_b_tr(t):
                    p4 = t % 4
                    for k0 in range(0, nf, 4):
                        hp = (k0 // 4) % 4
                        kn = min(4, nf - k0)
                        ctx.op("t", [tr(psPT3[hp][:, (k - k0) * 128:
                                            (k - k0 + 1) * 128],
                                        xqb[p4][:, k * 128:(k + 1) * 128],
                                        id_bf[:])
                                     for k in range(k0, k0 + kn)],
                               reads=[b_xqb[p4], b_consts], writes=[b_psPT3[hp]])

                def f_prep_b_cp(t):
                    p4 = t % 4
                    for k0 in range(0, nf, 4):
                        hp = (k0 // 4) % 4
                        kn = min(4, nf - k0)
                        ctx.op("v", cp(xqf[p4][:, k0 * 128:(k0 + kn) * 128],
                                       psPT3[hp][:, 0:kn * 128]),
                               reads=[b_psPT3[hp]], writes=[b_xqf[p4]])

                def f_compute(t):
                    p4 = t % 4
                    par = t % 2
                    st = stats[par]
                    bsp = b_stp[par]
                    for oc4 in range((d + 511) // 512):
                        w4 = min(512, d - oc4 * 512)
                        bk = (t % 2) * 2 + oc4 % 2
                        pp, bpp = psF[bk], b_psF[bk]
                        fns = [mm(pp[:, 0:w4], xqf[p4][:, ic * 128:(ic + 1) * 128],
                                  wf[:, ic * d + oc4 * 512:ic * d + oc4 * 512 + w4],
                                  ic == 0, ic == nf - 1) for ic in range(nf)]
                        ctx.op("t", fns, reads=[b_xqf[p4], b_wf], writes=[bpp])
                        ctx.op("v", stt(fb[par][:, oc4 * 512:oc4 * 512 + w4],
                                        pp[:, 0:w4], g2[:, 1:2],
                                        fbias[:, oc4 * 512:oc4 * 512 + w4],
                                        ALU.mult, ALU.add,
                                        accum=sp4f[par][:, oc4:oc4 + 1]),
                               reads=[bpp, b_g2, b_fbias],
                               writes=[b_fb[par], bsp])
                    ctx.op("v", red(st[0][:], sp4f[par][:, 0:(d + 511) // 512],
                                    ALU.add),
                           reads=[bsp], writes=[bsp])
                    ctx.op("a", act(xqb[p4][:], fb[par][:], AF.Square,
                                    accum=st[1][:]),
                           reads=[b_fb[par]], writes=[b_xqb[p4], bsp])
                    _ln_smalls(ctx, d, st[0][:], st[1][:], st[2][:], st[3][:],
                               st[4][:], epsb[:], bsp, b_eps)
                    ctx.op("v", stt(ot1[par][:], fb[par][:], st[3][:], xtf[p4][:],
                                    ALU.mult, ALU.add),
                           reads=[b_fb[par], bsp, b_xtf[p4]], writes=[b_ot1[par]])
                    ctx.op("a", act(ost[par][:], ot1[par][:], AF.Identity,
                                    bias=st[4][:]),
                           reads=[b_ot1[par], bsp], writes=[b_ost[par]])
                    seqi, ti = t // NT, t % NT
                    dma("y", f"do{par}", xout[seqi, ti * 128:(ti + 1) * 128, :],
                        ost[par][:], reads=[b_ost[par]], writes=[b_out])

                for i in range(cfg.NT_ALL + 2):
                    if i < cfg.NT_ALL:
                        f_prep_a(i)
                    if 1 <= i <= cfg.NT_ALL:
                        f_prep_b_cp(i - 1)
                    if i >= 2:
                        f_compute(i - 2)
                    if i < cfg.NT_ALL:
                        f_prep_b_tr(i)

        fin = []
        for sname in ("do0", "do1"):
            if sname in ctx.cnt:
                fin.append((sname, ctx.cnt[sname]))
        ctx.raw_wait("y", fin)

        handles = {}
        with ExitStack() as sem_es:
            for sname in ctx.semnames:
                handles[sname] = sem_es.enter_context(nc.semaphore(sname))
            with nc.Block() as block:
                @block.tensor
                def _(e):
                    ctx.emit(e, "t", handles)

                @block.vector
                def _(e):
                    ctx.emit(e, "v", handles)

                @block.scalar
                def _(e):
                    ctx.emit(e, "a", handles)

                @block.gpsimd
                def _(e):
                    ctx.emit(e, "g", handles)

                @block.sync
                def _(e):
                    ctx.emit(e, "y", handles)
    return nc


_BUILD_CACHE = {}


def _get_nc(cfg: Cfg):
    key = (cfg.B, cfg.S, cfg.d, cfg.H, cfg.KV, cfg.n_cores)
    if key not in _BUILD_CACHE:
        _BUILD_CACHE[key] = build(cfg)
    return _BUILD_CACHE[key]


def kernel(L0_attn_hidden_states, L0_attn_Wqkv_w, L0_attn_Wqkv_b,
           L0_attn_lambda_q1, L0_attn_lambda_k1, L0_attn_lambda_q2,
           L0_attn_lambda_k2, L0_attn_subln_w, L0_attn_out_w, L0_attn_out_b,
           L0_ffn_weight, L0_ffn_weight_scale, L0_ffn_bias):
    x = np.asarray(L0_attn_hidden_states)
    B, S, d = x.shape
    H = 16
    KV = 8
    cfg = Cfg(B=B, S=S, d=d, H=H, KV=KV, n_cores=8)
    nc = _get_nc(cfg)

    c = np.ascontiguousarray
    wqkvT = c(np.asarray(L0_attn_Wqkv_w).T)
    qkvb_t = c(np.asarray(L0_attn_Wqkv_b).reshape(cfg.OP // 128, 128).T)
    biasv_row = c(np.asarray(L0_attn_Wqkv_b)[H * 128 + KV * 128:].reshape(1, cfg.KVD))
    lam4 = c(np.stack([np.asarray(L0_attn_lambda_q1), np.asarray(L0_attn_lambda_k1),
                       np.asarray(L0_attn_lambda_q2), np.asarray(L0_attn_lambda_k2)],
                      axis=1))
    subln2 = c(np.asarray(L0_attn_subln_w).reshape(2, 128).T)
    outwT = c(np.asarray(L0_attn_out_w).T)
    outb_row = c(np.asarray(L0_attn_out_b).reshape(1, d))
    ffnT = c(np.asarray(L0_ffn_weight).T)
    ffnb_row = c(np.asarray(L0_ffn_bias).reshape(1, d))
    ws = c(np.asarray(L0_ffn_weight_scale).reshape(1, 1))

    in_maps = []
    for core in range(cfg.n_cores):
        in_maps.append({
            "xin": c(x[core * cfg.B_loc:(core + 1) * cfg.B_loc]),
            "wqkvT": wqkvT, "qkvb_t": qkvb_t, "biasv_row": biasv_row,
            "lam4": lam4, "subln2": subln2, "outwT": outwT, "outb_row": outb_row,
            "ffnT": ffnT, "ffnb_row": ffnb_row, "ws_in": ws,
        })
    res = run_bass_kernel_spmd(nc, in_maps, core_ids=list(range(cfg.n_cores)))
    return np.concatenate([r["xout"] for r in res.results], axis=0)
